# version 29
# baseline (speedup 1.0000x reference)
import numpy as np
from contextlib import ExitStack

import concourse.bass as bass
import concourse.mybir as mybir
from concourse.bass_utils import run_bass_kernel_spmd

F32 = mybir.dt.float32
BF16 = mybir.dt.bfloat16
I32 = mybir.dt.int32
AF = mybir.ActivationFunctionType
ALU = mybir.AluOpType
AX = mybir.AxisListType.X

P = 128
E = 1024
H = 4096
S_TOT = 16384
NCORES = 8
QOWN = S_TOT // NCORES
EC = E // P
DC = E // P
HC = H // P
NT_OWN = QOWN // P
NT_ALL = S_TOT // P
NG_ALL = NT_ALL // 4
KG = 1024
NKG = S_TOT // KG
QG = 512
NQG = QOWN // QG
KT_PG = KG // P
C_SHIFT = 40.0
EPS = 1e-5
QB = 127.0

ENGINES = ("sync", "tensor", "vector", "scalar", "gpsimd")
ESEM = {"sync": "SP", "tensor": "PE", "vector": "DVE", "scalar": "ACT",
        "gpsimd": "GP"}
DMA_POOL = {"LD_S": 8, "LD_G": 8, "LD_A": 8}
DMA_SEMS = tuple(f"{q}{i}" for q, r in DMA_POOL.items() for i in range(r))
SEMS = ("SP", "PE", "DVE", "ACT", "GP") + DMA_SEMS

_CFG = {"phases": 3, "debug_scratch": False}


class BufT:
    __slots__ = ("w", "r")

    def __init__(self):
        self.w = {}
        self.r = {}


def _merge(d, sem, n):
    if d.get(sem, 0) < n:
        d[sem] = n


class Plan:
    def __init__(self):
        self.ops = {e: [] for e in ENGINES}
        self.cnt = {s: 0 for s in SEMS}
        self.wm = {e: {} for e in ENGINES}
        self.rr = {q: 0 for q in DMA_POOL}

    def _deps(self, reads, writes, extra):
        deps = {}
        for b in reads:
            for s, c in b.w.items():
                _merge(deps, s, c)
        for b in writes:
            for s, c in b.w.items():
                _merge(deps, s, c)
            for s, c in b.r.items():
                _merge(deps, s, c)
        for s, c in extra:
            _merge(deps, s, c)
        return deps

    def _wait(self, engine, deps):
        wm = self.wm[engine]
        for sem, c in deps.items():
            if c <= 0 or wm.get(sem, 0) >= c:
                continue
            wm[sem] = c
            self.ops[engine].append(
                lambda eng, sems, s=sem, c=c: eng.wait_ge(sems[s], c))

    def _record(self, reads, writes, sem, n):
        for b in reads:
            _merge(b.r, sem, n)
        for b in writes:
            b.w = {sem: n}
            b.r = {}

    def op(self, engine, build, reads=(), writes=(), dma=None, extra=()):
        deps = self._deps(reads, writes, extra)
        if dma is not None:
            r = DMA_POOL[dma]
            sem = f"{dma}{self.rr[dma] % r}"
            self.rr[dma] += 1
            _merge(deps, sem, self.cnt[sem])
            inc = 16
        else:
            sem = ESEM[engine]
            inc = 1
        self._wait(engine, deps)
        self.cnt[sem] += inc
        n = self.cnt[sem]
        self.ops[engine].append(
            lambda eng, sems, build=build, s=sem, i=inc:
            build(eng).then_inc(sems[s], i))
        self._record(reads, writes, sem, n)
        return (sem, n)

    def group(self, engine, builds, reads=(), writes=(), extra=()):
        self._wait(engine, self._deps(reads, writes, extra))
        sem = ESEM[engine]
        for build in builds:
            self.cnt[sem] += 1
            self.ops[engine].append(
                lambda eng, sems, build=build, s=sem:
                build(eng).then_inc(sems[s], 1))
        n = self.cnt[sem]
        self._record(reads, writes, sem, n)
        return (sem, n)

    def barrier(self):
        counts = dict(self.cnt)
        for e in ENGINES:
            self._wait(e, counts)

    def finish(self, nc, sems):
        final = list(self.cnt.items())
        with nc.Block() as block:

            @block.sync
            def _(eng):
                for fn in self.ops["sync"]:
                    fn(eng, sems)
                for (s, c) in final:
                    if c > 0:
                        eng.wait_ge(sems[s], c)

            @block.tensor
            def _(eng):
                for fn in self.ops["tensor"]:
                    fn(eng, sems)

            @block.vector
            def _(eng):
                for fn in self.ops["vector"]:
                    fn(eng, sems)

            @block.scalar
            def _(eng):
                for fn in self.ops["scalar"]:
                    fn(eng, sems)

            @block.gpsimd
            def _(eng):
                for fn in self.ops["gpsimd"]:
                    fn(eng, sems)


def build_program():
    nc = bass.Bass("TRN2", target_bir_lowering=False, debug=False)
    skind = "ExternalOutput" if _CFG["debug_scratch"] else "Internal"

    x_in = nc.dram_tensor("x_in", [S_TOT, E], F32, kind="ExternalInput").ap()
    wqt = nc.dram_tensor("wqt", [E, E], F32, kind="ExternalInput").ap()
    wkt = nc.dram_tensor("wkt", [E, E], F32, kind="ExternalInput").ap()
    wvt = nc.dram_tensor("wvt", [E, E], F32, kind="ExternalInput").ap()
    w1t = nc.dram_tensor("w1t", [E, H], F32, kind="ExternalInput").ap()
    w2t = nc.dram_tensor("w2t", [H, E], F32, kind="ExternalInput").ap()
    bq_in = nc.dram_tensor("bq_in", [E], F32, kind="ExternalInput").ap()
    bk_in = nc.dram_tensor("bk_in", [E], F32, kind="ExternalInput").ap()
    bv_in = nc.dram_tensor("bv_in", [E], F32, kind="ExternalInput").ap()
    b1_in = nc.dram_tensor("b1_in", [H], F32, kind="ExternalInput").ap()
    b2_in = nc.dram_tensor("b2_in", [E], F32, kind="ExternalInput").ap()
    y_out = nc.dram_tensor("y_out", [QOWN, E], F32, kind="ExternalOutput").ap()

    xn_scr = nc.dram_tensor("xn_scr", [NT_ALL, P, E], BF16, kind="Internal").ap()
    x2_scr = nc.dram_tensor("x2_scr", [NT_OWN, P, E], F32, kind=skind).ap()
    w1q_scr = nc.dram_tensor("w1q_scr", [EC, P, H], BF16, kind=skind).ap()
    w2q_scr = nc.dram_tensor("w2q_scr", [HC, P, E], BF16, kind=skind).ap()
    l_scr = nc.dram_tensor("l_scr", [QOWN], F32, kind=skind).ap()
    dbg_xq = nc.dram_tensor("dbg_xq", [P, E], BF16, kind=skind).ap()
    dbg_h = nc.dram_tensor("dbg_h", [P, H], F32, kind=skind).ap()
    dbg_hqfm = nc.dram_tensor("dbg_hqfm", [P, HC, P], BF16, kind=skind).ap()
    dbg_xqfm = nc.dram_tensor("dbg_xqfm", [P, EC, P], BF16, kind=skind).ap()
    dbg_ls = nc.dram_tensor("dbg_ls", [P, 16], F32, kind=skind).ap()
    dbg_o2 = nc.dram_tensor("dbg_o2", [P, E], F32, kind=skind).ap()

    pl = Plan()
    B = {}

    def bt(name):
        if name not in B:
            B[name] = BufT()
        return B[name]

    with ExitStack() as ctx:
        _probe = nc.alloc_sbuf_tensor("_base_probe", [P, 8], F32)
        SB_BASE = None
        for _a in nc.m.functions[0].allocations:
            try:
                if _a.memorylocations[0].name == "_base_probe":
                    SB_BASE = int(_a.memorylocations[0].addr)
            except Exception:
                pass
        assert SB_BASE is not None and SB_BASE >= 4096, SB_BASE
        SB_LIMIT = SB_BASE + 212832
        cursor = [SB_BASE + 64]

        def at(name, shape, dtype, offset=None):
            if offset is not None:
                cursor[0] = offset
            off = (cursor[0] + 63) // 64 * 64
            nbytes = int(np.prod(shape[1:])) * mybir.dt.size(dtype)
            t = nc.alloc_sbuf_tensor_at(name, list(shape), dtype, offset=off)
            cursor[0] = off + nbytes
            return t.ap()

        ident = at("ident", [P, P], BF16)
        ones_row_bf = at("ones_row_bf", [1, P], BF16)
        p3_persist_end = cursor[0]
        ones_col = at("ones_col", [P, 1], BF16)
        ones_row_f = at("ones_row_f", [1, P], F32)
        ones_f32col = at("ones_f32col", [P, 1], F32)
        c_nshift = at("c_nshift", [P, 1], F32)
        ones_row5 = at("ones_row5", [1, 512], BF16)
        bq_row = at("bq_row", [1, E], BF16)
        bk_row = at("bk_row", [1, E], BF16)
        bv_bcast = at("bv_bcast", [P, E], BF16)
        linv_col = at("linv_col", [P, NT_OWN], F32)
        wsb = at("wsb", [P, 8], F32)
        ws_row = at("ws_row", [1, 8], F32)
        persist_end = cursor[0]

        q_fm = at("q_fm", [P, DC, QOWN], BF16)
        p1_base = cursor[0]

        wqt_sb = at("wqt_sb", [P, EC, E], BF16, offset=p1_base)
        wkt_sb = at("wkt_sb", [P, EC, E], BF16)
        x_tile = [at(f"x_tile{i}", [P, E], F32) for i in range(4)]
        xn_tm = [at(f"xn_tm{i}", [P, E], BF16) for i in range(3)]
        xn_fm = [at(f"xn_fm{i}", [P, EC, 512], BF16) for i in range(4)]
        k_stage = [at(f"k_stage{i}", [P, E], BF16) for i in range(4)]
        v_stage = [at(f"v_stage{i}", [P, E], BF16) for i in range(4)]
        ln_scr1 = [at(f"ln_scr1_{i}", [P, 16], F32) for i in range(4)]
        wq_stream = [at(f"wq_stream{i}", [P, H], F32) for i in range(3)]
        wq_sq = at("wq_sq", [P, 2048], F32)
        wq_out = at("wq_out", [P, H], BF16)
        asums = at("asums", [P, EC + HC + 2], F32)
        bq_col = at("bq_col", [P, EC], BF16)
        p1_end = cursor[0]

        z_acc = at("z_acc", [P, NT_OWN, E], F32, offset=p1_base)
        k_buf = [at(f"k_buf{i}", [P, DC, KG], BF16) for i in range(2)]
        v_buf = [at(f"v_buf{i}", [P, KT_PG, E], BF16) for i in range(2)]
        p_sb = at("p_sb", [P, KT_PG, QG], BF16)
        l_acc = at("l_acc", [1, QOWN], F32)
        xr_tile = [at(f"xr_tile{i}", [P, E], F32) for i in range(2)]
        ep_znbf = at("ep_znbf", [P, E], BF16)
        ep_znfm = [at("ep_znfm0", [P, EC, P], BF16)]
        x2_stage = [at(f"x2_stage{i}", [P, E], F32) for i in range(2)]
        ln_scr2 = [at(f"ln_scr2_{i}", [P, 16], F32) for i in range(2)]
        ep_znfm.append(at("ep_znfm1", [P, EC, P], BF16))
        p2_end = cursor[0]
        wv_ep = at("wv_ep", [P, EC, E], BF16, offset=p1_base + 98304)
        assert cursor[0] <= p1_base + 114688, cursor[0]

        w1q_sb = at("w1q_sb", [P, EC, H], BF16, offset=persist_end)
        w2q_sb = at("w2q_sb", [P, HC, E], BF16, offset=p1_base + 65536)
        b1_row = at("b1_row", [1, H], BF16)
        b2_row = at("b2_row", [1, E], BF16)
        cursor[0] = p1_base + 32768
        x2_tile = [at(f"x2_tile{i}", [P, E], F32) for i in range(2)]
        f1k = [at(f"f1k{i}", [P, E], F32) for i in range(3)]
        xqi_t = at("xqi_t", [P, E], I32)
        ln_scr3 = [at(f"ln_scr3_{i}", [P, 16], F32) for i in range(2)]
        hs_col = at("hs_col", [P, 8], F32)
        y_stage = [at("y_stage0", [P, E], F32)] * 2
        hq_bf = at("hq_bf", [P, E], BF16)
        thr_col = at("thr_col", [P, 4], F32)
        p3_endB = cursor[0]
        assert p3_endB <= p1_base + 65536, p3_endB
        cursor[0] = p1_base + 65536 + 65536 + 10240
        h_sb = at("h_sb", [P, H], F32)
        hq_fm = at("hq_fm", [P, HC, P], BF16)
        xq_bf = at("xq_bf", [P, E], BF16)
        xq_fm = at("xq_fm", [P, EC, P], BF16)
        p3_end = cursor[0]

        assert max(p1_end, p2_end, p3_end) <= SB_LIMIT, \
            (SB_BASE, p1_end, p2_end, p3_end)

        sems_ctx = ctx.enter_context(ExitStack())
        sems = {s: sems_ctx.enter_context(nc.semaphore(s)) for s in SEMS}

        def ms(ap_, val):
            return lambda eng: eng.memset(ap_, val)

        pl.op("gpsimd", ms(ident[:], 0.0), writes=[bt("ident")])
        pl.op("gpsimd", lambda eng: eng.affine_select(
            out=ident[:], in_=ident[:], compare_op=ALU.not_equal, fill=1.0,
            base=0, pattern=[[-1, P]], channel_multiplier=1),
            reads=[bt("ident")], writes=[bt("ident")])
        pl.op("gpsimd", ms(ones_col[:], 1.0), writes=[bt("ones_col")])
        pl.op("gpsimd", ms(ones_row_bf[:], 1.0), writes=[bt("ones_row_bf")])
        pl.op("gpsimd", ms(ones_row_f[:], 1.0), writes=[bt("ones_row_f")])
        pl.op("gpsimd", ms(ones_f32col[:], 1.0), writes=[bt("ones_f32col")])
        pl.op("gpsimd", ms(c_nshift[:], -C_SHIFT), writes=[bt("c_nshift")])

        pl.op("gpsimd", ms(ones_row5[:], 1.0), writes=[bt("ones_row5")])
        for ec in range(EC):
            pl.op("gpsimd", lambda eng, ec=ec: eng.dma_start(
                out=bq_col[:, ec:ec + 1],
                in_=bq_in[ec * P:(ec + 1) * P].rearrange("(p o) -> p o", o=1)),
                writes=[bt("bq_col")], dma="LD_G")
        pl.op("gpsimd", lambda eng: eng.dma_start(
            out=bv_bcast[0:1, :], in_=bv_in[:]),
            writes=[bt("bv_bcast")], dma="LD_G")
        for (src, dst, nm) in ((wkt, wkt_sb, "wkt_sb"),
                               (wqt, wqt_sb, "wqt_sb")):
            for ec in range(EC):
                pl.op("gpsimd", lambda eng, s=src, d=dst, ec=ec:
                      eng.dma_start(out=d[:, ec, :],
                                    in_=s[ec * P:(ec + 1) * P, :]),
                      writes=[bt(nm)], dma="LD_G")

        ps1 = ExitStack()
        ps_k = [ps1.enter_context(nc.psum_tensor(f"ps_k{i}", [P, E], F32))
                for i in range(2)]

        m_chunks = k_stage + v_stage

        def m_name(e_):
            return f"k_stage{e_}" if e_ < 4 else f"v_stage{e_ - 4}"

        for half in range(2):
            pl.group("tensor", [
                (lambda eng, sc=sc, half=half: eng.matmul(
                    ps_k[0][0:1, half * 512:(half + 1) * 512],
                    bq_col[:, sc:sc + 1],
                    wkt_sb[:, sc, half * 512:(half + 1) * 512],
                    start=(sc == 0), stop=(sc == EC - 1)))
                for sc in range(EC)],
                reads=[bt("bq_col"), bt("wkt_sb")], writes=[bt("ps_k0")])
        pl.op("vector", lambda eng: eng.tensor_copy(
            out=bq_row[0:1, :], in_=ps_k[0][0:1, :]),
            reads=[bt("ps_k0")], writes=[bt("bq_row")])
        for ec in range(EC):
            pp = ec % 2
            psb = bt(f"ps_k{pp}")
            builds = []
            for half in range(2):
                for sc in range(EC):
                    builds.append(
                        lambda eng, sc=sc, ec=ec, half=half, pp=pp:
                        eng.matmul(
                            ps_k[pp][:, half * 512:(half + 1) * 512],
                            wqt_sb[:, sc, ec * P:(ec + 1) * P],
                            wkt_sb[:, sc, half * 512:(half + 1) * 512],
                            start=(sc == 0), stop=(sc == EC - 1)))
            pl.group("tensor", builds,
                     reads=[bt("wqt_sb"), bt("wkt_sb")], writes=[psb])
            pl.op("scalar", lambda eng, ec=ec, pp=pp: eng.copy(
                out=m_chunks[ec][:], in_=ps_k[pp][:]),
                reads=[psb], writes=[bt(m_name(ec))])

        wq_tiles = [("w1", i) for i in range(EC)] + \
                   [("w2", i) for i in range(HC)]
        wq_a_idx = [0]

        wq_a_loaded = [0]

        def wq_load(idx):
            if idx >= len(wq_tiles) or idx < wq_a_loaded[0]:
                return
            wq_a_loaded[0] = idx + 1
            which, i = wq_tiles[idx]
            par = idx % 3
            src_t = w1t if which == "w1" else w2t
            cols = H if which == "w1" else E
            e_, q_ = ("gpsimd", "LD_G") if idx % 2 else ("sync", "LD_S")
            pl.op(e_, lambda eng, s=src_t, i=i, par=par, cols=cols:
                  eng.dma_start(out=wq_stream[par][:, 0:cols],
                                in_=s[i * P:(i + 1) * P, :]),
                  writes=[bt(f"wq_stream{par}")], dma=q_)

        def emit_wquant_a(count):
            for _ in range(count):
                if wq_a_idx[0] >= len(wq_tiles):
                    return
                idx = wq_a_idx[0]
                wq_a_idx[0] += 1
                wq_load(idx)
                wq_load(idx + 1)
                wq_load(idx + 2)
                which, i = wq_tiles[idx]
                par = idx % 3
                cols = H if which == "w1" else E
                sbuf = bt(f"wq_stream{par}")
                pl.op("vector", lambda eng, par=par, cols=cols, col=idx:
                      eng.reduce_sum(out=asums[:, col:col + 1],
                                     in_=wq_stream[par][:, 0:cols], axis=AX,
                                     apply_absolute_value=True),
                      reads=[sbuf], writes=[bt("asums")])

        def emit_wquant_combine():
            pl.op("vector", lambda eng: eng.reduce_sum(
                out=asums[:, EC + HC:EC + HC + 1], in_=asums[:, 0:EC],
                axis=AX), reads=[bt("asums")], writes=[bt("asums")])
            pl.op("vector", lambda eng: eng.reduce_sum(
                out=asums[:, EC + HC + 1:EC + HC + 2],
                in_=asums[:, EC:EC + HC], axis=AX),
                reads=[bt("asums")], writes=[bt("asums")])
            pl.group("tensor", [
                lambda eng: eng.matmul(
                    ps_k[0][0:1, 0:1], asums[:, EC + HC:EC + HC + 1],
                    ones_f32col[:], start=True, stop=True),
                lambda eng: eng.matmul(
                    ps_k[0][0:1, 1:2], asums[:, EC + HC + 1:EC + HC + 2],
                    ones_f32col[:], start=True, stop=True)],
                reads=[bt("asums"), bt("ones_f32col")], writes=[bt("ps_k0")])
            pl.op("vector", lambda eng: eng.tensor_scalar(
                out=ws_row[0:1, 0:1], in0=ps_k[0][0:1, 0:1],
                scalar1=1.0 / (H * E), scalar2=EPS, op0=ALU.mult,
                op1=ALU.max),
                reads=[bt("ps_k0")], writes=[bt("ws_row")])
            pl.op("vector", lambda eng: eng.tensor_scalar(
                out=ws_row[0:1, 3:4], in0=ps_k[0][0:1, 1:2],
                scalar1=1.0 / (H * E), scalar2=EPS, op0=ALU.mult,
                op1=ALU.max),
                reads=[bt("ps_k0")], writes=[bt("ws_row")])
            for (src_c, dst_c) in ((0, 1), (3, 4)):
                pl.op("vector", lambda eng, s=src_c, d=dst_c:
                      eng.tensor_single_scalar(
                          out=ws_row[0:1, d:d + 1], in_=ws_row[0:1, s:s + 1],
                          scalar=0.5, op=ALU.mult),
                      reads=[bt("ws_row")], writes=[bt("ws_row")])
            for (src_c, dst_c) in ((1, 2), (4, 5), (0, 6), (3, 7)):
                pl.op("vector", lambda eng, s=src_c, d=dst_c:
                      eng.tensor_single_scalar(
                          out=ws_row[0:1, d:d + 1], in_=ws_row[0:1, s:s + 1],
                          scalar=-1.0, op=ALU.mult),
                      reads=[bt("ws_row")], writes=[bt("ws_row")])
            pl.group("tensor", [lambda eng: eng.matmul(
                ps_k[0][:, 0:8], ones_row_f[:], ws_row[:], start=True,
                stop=True)],
                reads=[bt("ws_row"), bt("ones_row_f")], writes=[bt("ps_k0")])
            pl.op("vector", lambda eng: eng.tensor_copy(
                out=wsb[:], in_=ps_k[0][:, 0:8]),
                reads=[bt("ps_k0")], writes=[bt("wsb")])

        wq_b_idx = [0]

        wq_b_loaded = [0]

        def wq_load_b(idx):
            if idx >= len(wq_tiles) or idx < wq_b_loaded[0]:
                return
            wq_b_loaded[0] = idx + 1
            which, i = wq_tiles[idx]
            par = idx % 3
            src_t = w1t if which == "w1" else w2t
            cols = H if which == "w1" else E
            e_, q_ = ("gpsimd", "LD_G") if idx % 2 else ("sync", "LD_S")
            pl.op(e_, lambda eng, s=src_t, i=i, par=par, cols=cols:
                  eng.dma_start(out=wq_stream[par][:, 0:cols],
                                in_=s[i * P:(i + 1) * P, :]),
                  writes=[bt(f"wq_stream{par}")], dma=q_)

        def emit_wquant_b(count):
            for _ in range(count):
                if wq_b_idx[0] >= len(wq_tiles):
                    return
                idx = wq_b_idx[0]
                which, i = wq_tiles[idx]
                wq_b_idx[0] += 1
                wq_load_b(idx)
                wq_load_b(idx + 1)
                wq_load_b(idx + 2)
                par = idx % 3
                dst = w1q_scr if which == "w1" else w2q_scr
                dstb = bt("w1q_scr") if which == "w1" else bt("w2q_scr")
                cols = H if which == "w1" else E
                o = 0 if which == "w1" else 3
                sbuf = bt(f"wq_stream{par}")
                nws_c = 6 if which == "w1" else 7
                for hh in range(2 if cols == H else 1):
                    lo = hh * 2048
                    w = min(cols, lo + 2048) - lo
                    pl.op("vector", lambda eng, par=par, lo=lo, w=w, o=o:
                          eng.tensor_scalar(
                              out=wq_sq[:, 0:w],
                              in0=wq_stream[par][:, lo:lo + w],
                              scalar1=wsb[:, o + 1:o + 2], scalar2=None,
                              op0=ALU.is_ge),
                          reads=[sbuf, bt("wsb")], writes=[bt("wq_sq")])
                    pl.op("vector", lambda eng, par=par, lo=lo, w=w, o=o:
                          eng.scalar_tensor_tensor(
                              out=wq_sq[:, 0:w],
                              in0=wq_stream[par][:, lo:lo + w],
                              scalar=wsb[:, o + 2:o + 3], in1=wq_sq[:, 0:w],
                              op0=ALU.is_le, op1=ALU.subtract),
                          reads=[sbuf, bt("wsb"), bt("wq_sq")],
                          writes=[bt("wq_sq")])
                    pl.op("vector", lambda eng, lo=lo, w=w, nws_c=nws_c:
                          eng.tensor_scalar(
                              out=wq_out[:, lo:lo + w], in0=wq_sq[:, 0:w],
                              scalar1=wsb[:, nws_c:nws_c + 1], scalar2=None,
                              op0=ALU.mult),
                          reads=[bt("wq_sq"), bt("wsb")],
                          writes=[bt("wq_out")])
                pl.op("scalar", lambda eng, d=dst, i=i, cols=cols:
                      eng.dma_start(out=d[i], in_=wq_out[:, 0:cols]),
                      reads=[bt("wq_out")], writes=[dstb], dma="LD_A")

        def x_load(t):
            par = t % 4
            pl.op("scalar", lambda eng, t=t, par=par: eng.dma_start(
                out=x_tile[par][:], in_=x_in[t * P:(t + 1) * P, :]),
                writes=[bt(f"x_tile{par}")], dma="LD_A")

        def xn_load(g):
            xp = g % 4
            xfm = bt(f"xn_fm{xp}")
            srcs = [bt(f"xn_scr_t{g * 4 + i}") for i in range(4)]
            for e_ in range(EC):
                src_ap = xn_scr[g * 4:(g + 1) * 4, :,
                                e_ * P:(e_ + 1) * P].rearrange(
                                    "a p d -> (a p) d")
                pl.op("sync", lambda eng, e_=e_, xp=xp, src_ap=src_ap:
                      eng.dma_start_transpose(xn_fm[xp][:, e_, :], src_ap),
                      reads=srcs, writes=[xfm], dma="LD_S")

        def emit_ln(eng_scr, scr_b, src_ap, src_b):
            pl.op("vector", lambda eng: eng.bn_stats(
                out=eng_scr[:, 0:6], in_=src_ap[:, 0:512]),
                reads=[src_b], writes=[scr_b])
            pl.op("vector", lambda eng: eng.bn_stats(
                out=eng_scr[:, 6:12], in_=src_ap[:, 512:1024]),
                reads=[src_b], writes=[scr_b])
            pl.op("vector", lambda eng: eng.bn_aggr(
                out=eng_scr[:, 12:14], in_=eng_scr[:, 0:12]),
                reads=[scr_b], writes=[scr_b])
            pl.op("vector", lambda eng: eng.tensor_scalar_add(
                out=eng_scr[:, 14:15], in0=eng_scr[:, 13:14], scalar1=EPS),
                reads=[scr_b], writes=[scr_b])
            pl.op("scalar", lambda eng: eng.activation(
                out=eng_scr[:, 14:15], in_=eng_scr[:, 14:15], func=AF.Sqrt),
                reads=[scr_b], writes=[scr_b])
            pl.op("vector", lambda eng: eng.reciprocal(
                out=eng_scr[:, 14:15], in_=eng_scr[:, 14:15]),
                reads=[scr_b], writes=[scr_b])
            pl.op("vector", lambda eng: eng.scalar_tensor_tensor(
                out=eng_scr[:, 15:16], in0=eng_scr[:, 12:13], scalar=-1.0,
                in1=eng_scr[:, 14:15], op0=ALU.mult, op1=ALU.mult),
                reads=[scr_b], writes=[scr_b])

        for t in range(4):
            x_load(t)
        for t in range(NT_ALL):
            xpar = t % 4
            par = t % 3
            ls = ln_scr1[t % 4]
            lsb = bt(f"ln_scr1_{t % 4}")
            xb = bt(f"x_tile{xpar}")
            emit_ln(ls, lsb, x_tile[xpar], xb)
            pl.op("vector", lambda eng, xpar=xpar, par=par, ls=ls:
                  eng.tensor_scalar(
                out=xn_tm[par][:], in0=x_tile[xpar][:],
                scalar1=ls[:, 14:15], scalar2=ls[:, 15:16],
                op0=ALU.mult, op1=ALU.add),
                reads=[xb, lsb], writes=[bt(f"xn_tm{par}")])
            if t + 4 < NT_ALL:
                x_load(t + 4)
            pl.op("scalar", lambda eng, t=t, par=par: eng.dma_start(
                out=xn_scr[t], in_=xn_tm[par][:]),
                reads=[bt(f"xn_tm{par}")], writes=[bt(f"xn_scr_t{t}")],
                dma="LD_A")
            if t >= 8 and t % 2 == 1:
                emit_wquant_a(1)
            if t == 105:
                emit_wquant_a(100)
                emit_wquant_combine()
            if t >= 107 and t % 2 == 1:
                emit_wquant_b(1)
            if t % 4 == 3 and t // 4 < 4:
                xn_load(t // 4)

        for g in range(NT_OWN // 4):
            xp = g % 4
            xfm = bt(f"xn_fm{xp}")
            for dc in range(DC):
                psb = bt("ps_k0")
                builds = [
                    (lambda eng, e_=e_, dc=dc, xp=xp:
                     eng.matmul(ps_k[0][:, 0:512],
                                m_chunks[e_][:, dc * P:(dc + 1) * P],
                                xn_fm[xp][:, e_, :],
                                start=(e_ == 0), stop=False))
                    for e_ in range(EC)]
                builds.append(
                    lambda eng, dc=dc: eng.matmul(
                        ps_k[0][:, 0:512],
                        bq_row[0:1, dc * P:(dc + 1) * P],
                        ones_row5[:], start=False, stop=True))
                pl.group("tensor", builds,
                         reads=[bt(m_name(e_)) for e_ in range(EC)] +
                               [xfm, bt("bq_row"), bt("ones_row5")],
                         writes=[psb])
                pl.op("scalar", lambda eng, dc=dc, g=g:
                      eng.copy(
                          out=q_fm[:, dc, g * 512:(g + 1) * 512],
                          in_=ps_k[0][:, 0:512]),
                      reads=[psb], writes=[bt("q_fm")])
                emit_wquant_b(2)
        emit_wquant_b(100)

        ps1.close()
        if _CFG["phases"] == 1:
            pl.barrier()
            pl.finish(nc, sems)
            return nc

        pl.barrier()

        ps2 = ExitStack()
        ps_s = [ps2.enter_context(nc.psum_tensor(f"ps_s{i}", [P, QG], F32))
                for i in range(2)]
        ps_z = [ps2.enter_context(nc.psum_tensor(f"ps_z{i}", [P, 512], F32))
                for i in range(4)]
        ps_l = ps2.enter_context(nc.psum_tensor("ps_l", [1, QG], F32))

        def load_kgroup(kg):
            par = kg % 2
            kb, vb = bt(f"k_buf{par}"), bt(f"v_buf{par}")
            srcs = [bt(f"xn_scr_t{kg * KT_PG + i}") for i in range(KT_PG)]
            for dc in range(DC):
                src_ap = xn_scr[kg * KT_PG:(kg + 1) * KT_PG, :,
                                dc * P:(dc + 1) * P].rearrange(
                                    "a p d -> (a p) d")
                pl.op("sync", lambda eng, dc=dc, par=par, src_ap=src_ap:
                      eng.dma_start_transpose(k_buf[par][:, dc, :], src_ap),
                      reads=srcs, writes=[kb], dma="LD_S")
            for kt in range(KT_PG):
                t = kg * KT_PG + kt
                pl.op("sync", lambda eng, kt=kt, t=t, par=par:
                      eng.dma_start(out=v_buf[par][:, kt, :], in_=xn_scr[t]),
                      reads=[bt(f"xn_scr_t{t}")], writes=[vb], dma="LD_S")

        load_kgroup(0)
        load_kgroup(1)

        for kg in range(NKG):
            par = kg % 2
            kb, vb = bt(f"k_buf{par}"), bt(f"v_buf{par}")
            if kg == NKG - 1:
                wv_gate = list(dict(pl.cnt).items())
                for ec in range(EC):
                    pl.op("gpsimd", lambda eng, ec=ec: eng.dma_start(
                        out=wv_ep[:, ec, :], in_=wvt[ec * P:(ec + 1) * P, :]),
                        writes=[bt("wv_ep")], dma="LD_G", extra=wv_gate)
            for qg in range(NQG):
                for kt in range(KT_PG):
                    sp = kt % 2
                    pss = bt(f"ps_s{sp}")
                    pl.group("tensor", [
                        (lambda eng, dc=dc, kt=kt, sp=sp, par=par, qg=qg:
                         eng.matmul(ps_s[sp][:],
                                    k_buf[par][:, dc, kt * P:(kt + 1) * P],
                                    q_fm[:, dc, qg * QG:(qg + 1) * QG],
                                    start=(dc == 0), stop=(dc == DC - 1)))
                        for dc in range(DC)],
                        reads=[kb, bt("q_fm")], writes=[pss])
                    pl.op("scalar", lambda eng, kt=kt, sp=sp: eng.activation(
                        out=p_sb[:, kt, :], in_=ps_s[sp][:], func=AF.Exp,
                        scale=1.0 / 32.0, bias=c_nshift[:]),
                        reads=[pss, bt("c_nshift")], writes=[bt("p_sb")])
                for qt in range(4):
                    psz = bt(f"ps_z{qt}")
                    pl.group("tensor", [
                        (lambda eng, kt=kt, qt=qt, par=par:
                         eng.matmul(ps_z[qt][:],
                                    p_sb[:, kt, qt * P:(qt + 1) * P],
                                    v_buf[par][:, kt, 0:512],
                                    start=(kt == 0), stop=(kt == KT_PG - 1)))
                        for kt in range(KT_PG)],
                        reads=[bt("p_sb"), vb], writes=[psz])
                pl.group("tensor", [
                    (lambda eng, kt=kt: eng.matmul(
                        ps_l[:], ones_col[:], p_sb[:, kt, :],
                        start=(kt == 0), stop=(kt == KT_PG - 1)))
                    for kt in range(KT_PG)],
                    reads=[bt("p_sb"), bt("ones_col")], writes=[bt("ps_l")])
                for qt in range(4):
                    psz = bt(f"ps_z{qt}")
                    t_own = qg * 4 + qt
                    if kg == 0:
                        pl.op("vector", lambda eng, qt=qt, t=t_own:
                              eng.tensor_copy(out=z_acc[:, t, 0:512],
                                              in_=ps_z[qt][:]),
                              reads=[psz], writes=[bt(f"z_acc{t_own}")])
                    else:
                        pl.op("vector", lambda eng, qt=qt, t=t_own:
                              eng.tensor_add(out=z_acc[:, t, 0:512],
                                             in0=ps_z[qt][:],
                                             in1=z_acc[:, t, 0:512]),
                              reads=[psz, bt(f"z_acc{t_own}")],
                              writes=[bt(f"z_acc{t_own}")])
                if kg == 0:
                    pl.op("vector", lambda eng, qg=qg: eng.tensor_copy(
                        out=l_acc[0:1, qg * QG:(qg + 1) * QG], in_=ps_l[:]),
                        reads=[bt("ps_l")], writes=[bt(f"l_acc{qg}")])
                else:
                    pl.op("vector", lambda eng, qg=qg: eng.tensor_add(
                        out=l_acc[0:1, qg * QG:(qg + 1) * QG], in0=ps_l[:],
                        in1=l_acc[0:1, qg * QG:(qg + 1) * QG]),
                        reads=[bt("ps_l"), bt(f"l_acc{qg}")],
                        writes=[bt(f"l_acc{qg}")])
                for qt in range(4):
                    psz = bt(f"ps_z{qt}")
                    t_own = qg * 4 + qt
                    pl.group("tensor", [
                        (lambda eng, kt=kt, qt=qt, par=par:
                         eng.matmul(ps_z[qt][:],
                                    p_sb[:, kt, qt * P:(qt + 1) * P],
                                    v_buf[par][:, kt, 512:1024],
                                    start=(kt == 0), stop=(kt == KT_PG - 1)))
                        for kt in range(KT_PG)],
                        reads=[bt("p_sb"), vb], writes=[psz])
                for qt in range(4):
                    psz = bt(f"ps_z{qt}")
                    t_own = qg * 4 + qt
                    if kg == 0:
                        pl.op("vector", lambda eng, qt=qt, t=t_own:
                              eng.tensor_copy(out=z_acc[:, t, 512:1024],
                                              in_=ps_z[qt][:]),
                              reads=[psz], writes=[bt(f"z_acc{t_own}")])
                    else:
                        pl.op("vector", lambda eng, qt=qt, t=t_own:
                              eng.tensor_add(out=z_acc[:, t, 512:1024],
                                             in0=ps_z[qt][:],
                                             in1=z_acc[:, t, 512:1024]),
                              reads=[psz, bt(f"z_acc{t_own}")],
                              writes=[bt(f"z_acc{t_own}")])
            if kg + 2 < NKG:
                load_kgroup(kg + 2)

        att_end = dict(pl.cnt)
        zn_gates = {}
        la = [bt(f"l_acc{qg}") for qg in range(NQG)]
        pl.op("vector", lambda eng: eng.reciprocal(out=l_acc[:], in_=l_acc[:]),
              reads=la, writes=la)
        pl.op("gpsimd", lambda eng: eng.dma_start(
            out=l_scr.rearrange("(o q) -> o q", o=1), in_=l_acc[0:1, :]),
            reads=la, writes=[bt("l_scr")], dma="LD_G")
        for tt in range(NT_OWN):
            pl.op("gpsimd", lambda eng, tt=tt: eng.dma_start(
                out=linv_col[:, tt:tt + 1], in_=l_scr[tt * P:(tt + 1) * P]),
                reads=[bt("l_scr")], writes=[bt("linv_col")], dma="LD_G")

        ps2.close()
        ps2e = ExitStack()
        ps_tz = [ps2e.enter_context(nc.psum_tensor(f"ps_tz{i}", [P, E], BF16))
                 for i in range(2)]
        ps_pj = [ps2e.enter_context(nc.psum_tensor(f"ps_pj{i}", [P, E], F32))
                 for i in range(2)]
        att_gateE = list(att_end.items())

        def xr_load(t):
            par = t % 2
            pl.op("scalar", lambda eng, t=t, par=par: eng.dma_start(
                out=xr_tile[par][:], in_=x_in[t * P:(t + 1) * P, :]),
                writes=[bt(f"xr_tile{par}")], dma="LD_A")

        xr_load(0)
        xr_load(1)

        def ep_zn(t):
            par = t % 2
            pl.op("vector", lambda eng, t=t: eng.tensor_scalar(
                out=ep_znbf[:], in0=z_acc[:, t, :],
                scalar1=linv_col[:, t:t + 1], scalar2=None, op0=ALU.mult),
                reads=[bt(f"z_acc{t}"), bt("linv_col")],
                writes=[bt("ep_znbf")],
                extra=(att_gateE if t == 0 else ()))
            zn_gates[t] = ("DVE", pl.cnt["DVE"])
            pl.group("tensor", [
                (lambda eng, e_=e_, par=par: eng.transpose(
                    ps_tz[par][:, e_ * P:(e_ + 1) * P],
                    ep_znbf[:, e_ * P:(e_ + 1) * P], ident[:]))
                for e_ in range(EC)],
                reads=[bt("ep_znbf"), bt("ident")],
                writes=[bt(f"ps_tz{par}")],
                extra=(att_gateE if t < 2 else ()))
            pl.op("vector", lambda eng, par=par: eng.tensor_copy(
                out=ep_znfm[par][:],
                in_=ps_tz[par].rearrange("p (c q) -> p c q", c=EC)),
                reads=[bt(f"ps_tz{par}")], writes=[bt(f"ep_znfm{par}")],
                extra=(att_gateE if t < 2 else ()))

        def ep_proj(t):
            par = t % 2
            builds = []
            for half in range(2):
                for e_ in range(EC):
                    builds.append(
                        lambda eng, e_=e_, half=half, par=par: eng.matmul(
                            ps_pj[par][:, half * 512:(half + 1) * 512],
                            ep_znfm[par][:, e_, :],
                            wv_ep[:, e_, half * 512:(half + 1) * 512],
                            start=(e_ == 0), stop=False))
                builds.append(
                    lambda eng, half=half, par=par: eng.matmul(
                        ps_pj[par][:, half * 512:(half + 1) * 512],
                        ones_row_bf[:],
                        bv_bcast[0:1, half * 512:(half + 1) * 512],
                        start=False, stop=True))
            pl.group("tensor", builds,
                     reads=[bt(f"ep_znfm{par}"), bt("wv_ep"),
                            bt("ones_row_bf"), bt("bv_bcast")],
                     writes=[bt(f"ps_pj{par}")],
                     extra=(att_gateE if t < 2 else ()))

        def ep_out(t):
            par = t % 2
            ls = ln_scr2[t % 2]
            lsb = bt(f"ln_scr2_{t % 2}")
            emit_ln(ls, lsb, ps_pj[par], bt(f"ps_pj{par}"))
            sp = t % 2
            pl.op("vector", lambda eng, ls=ls, sp=sp, par=par:
                  eng.tensor_scalar(
                      out=x2_stage[sp][:], in0=ps_pj[par][:],
                      scalar1=ls[:, 14:15], scalar2=ls[:, 15:16],
                      op0=ALU.mult, op1=ALU.add),
                  reads=[bt(f"ps_pj{par}"), lsb], writes=[bt(f"x2_stage{sp}")])
            pl.op("vector", lambda eng, par=par, sp=sp: eng.tensor_add(
                out=x2_stage[sp][:], in0=x2_stage[sp][:],
                in1=xr_tile[par][:]),
                reads=[bt(f"x2_stage{sp}"), bt(f"xr_tile{par}")],
                writes=[bt(f"x2_stage{sp}")])
            if t + 2 < NT_OWN:
                xr_load(t + 2)
            pl.op("gpsimd", lambda eng, t=t, sp=sp: eng.dma_start(
                out=x2_scr[t], in_=x2_stage[sp][:]),
                reads=[bt(f"x2_stage{sp}")], writes=[bt(f"x2_scr_t{t}")],
                dma="LD_G")

        ep_zn(0)
        ep_proj(0)
        gate_pj = [None]
        for t in range(NT_OWN):
            if t + 1 < NT_OWN:
                ep_zn(t + 1)
                ep_proj(t + 1)
                if t + 1 == NT_OWN - 1:
                    gate_pj[0] = dict(pl.cnt)
            ep_out(t)
        ps2e.close()
        if _CFG["phases"] == 2:
            pl.barrier()
            pl.finish(nc, sems)
            return nc

        pre_wload = dict(pl.cnt)
        att_gate = list(att_end.items())
        pj_done = list(gate_pj[0].items())
        for i in range(HC):
            g2 = pj_done if 16 <= i < 24 else att_gate
            pl.op("sync", lambda eng, i=i: eng.dma_start(
                out=w2q_sb[:, i, :], in_=w2q_scr[i]),
                reads=[bt("w2q_scr")], writes=[bt("w2q_sb")], dma="LD_S",
                extra=g2)
        pl.op("gpsimd", lambda eng: eng.dma_start(out=b1_row[:], in_=b1_in[:]),
              writes=[bt("b1_row")], dma="LD_G", extra=att_gate)
        pl.op("gpsimd", lambda eng: eng.dma_start(out=b2_row[:], in_=b2_in[:]),
              writes=[bt("b2_row")], dma="LD_G", extra=att_gate)
        w1_gate = att_gate + [zn_gates[8]]
        for i in range(EC):
            pl.op("sync", lambda eng, i=i: eng.dma_start(
                out=w1q_sb[:, i, :], in_=w1q_scr[i]),
                reads=[bt("w1q_scr")], writes=[bt("w1q_sb")], dma="LD_S",
                extra=w1_gate)
        for e in ("tensor", "vector", "scalar", "gpsimd"):
            pl._wait(e, pre_wload)

        ps3 = ExitStack()
        ps_h = [ps3.enter_context(nc.psum_tensor(f"ps_h{i}", [P, 512], F32))
                for i in range(2)]
        ps_t3 = [ps3.enter_context(nc.psum_tensor(f"ps_t3{i}", [P, E], BF16))
                 for i in range(2)]
        ps_o2 = [ps3.enter_context(nc.psum_tensor(f"ps_o2{i}", [P, E], F32))
                 for i in range(2)]

        ep_gate = list(pre_wload.items())

        def x2_load(t):
            par = t % 2
            pl.op("sync", lambda eng, t=t, par=par: eng.dma_start(
                out=x2_tile[par][:], in_=x2_scr[t]),
                reads=[bt(f"x2_scr_t{t}")], writes=[bt(f"x2_tile{par}")],
                dma="LD_S", extra=[zn_gates[8 + par]])

        pl.op("gpsimd", lambda eng: eng.memset(thr_col[:, 0:1], -2.25),
              writes=[bt("thr_col")])
        pl.op("gpsimd", lambda eng: eng.memset(thr_col[:, 1:2], -6.25),
              writes=[bt("thr_col")])

        x2_load(0)
        x2_load(1)

        def emit_round(src_ap, sb, int_ap, ib, d_ap, db, sq_ap, qb,
                       r_ap, rb):
            pl.op("vector", lambda eng: eng.tensor_copy(out=int_ap, in_=src_ap),
                  reads=[sb], writes=[ib])
            pl.op("vector", lambda eng: eng.scalar_tensor_tensor(
                out=d_ap, in0=int_ap, scalar=-1.0, in1=src_ap,
                op0=ALU.mult, op1=ALU.add),
                reads=[ib, sb], writes=[db])
            pl.op("scalar", lambda eng: eng.activation(
                out=sq_ap, in_=d_ap, func=AF.Square),
                reads=[db], writes=[qb])
            pl.op("scalar", lambda eng: eng.activation(
                out=d_ap, in_=d_ap, func=AF.Sign),
                reads=[db], writes=[db])
            pl.op("vector", lambda eng: eng.tensor_single_scalar(
                out=src_ap, in_=sq_ap, scalar=0.25, op=ALU.is_ge),
                reads=[qb], writes=[sb])
            pl.op("vector", lambda eng: eng.scalar_tensor_tensor(
                out=src_ap, in0=sq_ap, scalar=2.25, in1=src_ap,
                op0=ALU.is_ge, op1=ALU.add),
                reads=[qb, sb], writes=[sb])
            pl.op("vector", lambda eng: eng.scalar_tensor_tensor(
                out=src_ap, in0=sq_ap, scalar=6.25, in1=src_ap,
                op0=ALU.is_ge, op1=ALU.add),
                reads=[qb, sb], writes=[sb])
            pl.op("vector", lambda eng: eng.tensor_mul(
                out=sq_ap, in0=src_ap, in1=d_ap),
                reads=[sb, db], writes=[qb])
            pl.op("vector", lambda eng: eng.tensor_add(
                out=r_ap, in0=int_ap, in1=sq_ap),
                reads=[ib, qb], writes=[rb])

        def stage_x(t):
            par = t % 2
            xb = bt(f"x2_tile{par}")
            ls = ln_scr3[t % 2]
            lsb = bt(f"ln_scr3_{t % 2}")
            emit_ln(ls, lsb, x2_tile[par], xb)
            pl.op("vector", lambda eng, par=par, ls=ls: eng.tensor_scalar(
                out=f1k[0][:], in0=x2_tile[par][:],
                scalar1=ls[:, 14:15], scalar2=ls[:, 15:16],
                op0=ALU.mult, op1=ALU.add),
                reads=[xb, lsb], writes=[bt("f1k0")])
            pl.op("vector", lambda eng: eng.reduce_max(
                out=hs_col[:, 0:1], in_=f1k[0][:], axis=AX,
                apply_absolute_value=True),
                reads=[bt("f1k0")], writes=[bt("hs_col")])
            pl.op("vector", lambda eng: eng.tensor_scalar_max(
                out=hs_col[:, 0:1], in0=hs_col[:, 0:1], scalar1=EPS),
                reads=[bt("hs_col")], writes=[bt("hs_col")])
            pl.op("vector", lambda eng: eng.reciprocal(
                out=hs_col[:, 1:2], in_=hs_col[:, 0:1]),
                reads=[bt("hs_col")], writes=[bt("hs_col")])
            pl.op("vector", lambda eng: eng.tensor_scalar_mul(
                out=hs_col[:, 1:2], in0=hs_col[:, 1:2], scalar1=QB),
                reads=[bt("hs_col")], writes=[bt("hs_col")])
            pl.op("vector", lambda eng: eng.tensor_scalar_mul(
                out=hs_col[:, 2:3], in0=hs_col[:, 0:1], scalar1=1.0 / QB),
                reads=[bt("hs_col")], writes=[bt("hs_col")])
            pl.op("vector", lambda eng: eng.tensor_scalar(
                out=f1k[1][:], in0=f1k[0][:], scalar1=hs_col[:, 1:2],
                scalar2=None, op0=ALU.mult),
                reads=[bt("f1k0"), bt("hs_col")], writes=[bt("f1k1")])
            emit_round(f1k[1][:], bt("f1k1"), xqi_t[:], bt("xqi"),
                       f1k[2][:], bt("f1k2"), f1k[0][:], bt("f1k0"),
                       f1k[1][:], bt("f1k1"))
            pl.op("vector", lambda eng: eng.tensor_scalar(
                out=xq_bf[:], in0=f1k[1][:], scalar1=hs_col[:, 2:3],
                scalar2=None, op0=ALU.mult),
                reads=[bt("f1k1"), bt("hs_col")], writes=[bt("xq_bf")])
            if t == 0 and _CFG["debug_scratch"]:
                pl.op("gpsimd", lambda eng: eng.dma_start(
                    out=dbg_xq[:], in_=xq_bf[:]),
                    reads=[bt("xq_bf")], writes=[bt("dbg")], dma="LD_G")
            tp = t % 2
            pst = bt(f"ps_t3{tp}")
            pl.group("tensor", [
                (lambda eng, e_=e_, tp=tp: eng.transpose(
                    ps_t3[tp][:, e_ * P:(e_ + 1) * P],
                    xq_bf[:, e_ * P:(e_ + 1) * P], ident[:]))
                for e_ in range(EC)],
                reads=[bt("xq_bf"), bt("ident")], writes=[pst])
            pl.op("scalar", lambda eng, tp=tp: eng.copy(
                out=xq_fm[:],
                in_=ps_t3[tp].rearrange("p (c q) -> p c q", c=EC)),
                reads=[pst], writes=[bt("xq_fm")])
            if t == 0 and _CFG["debug_scratch"]:
                pl.op("gpsimd", lambda eng: eng.dma_start(
                    out=dbg_xqfm[:], in_=xq_fm[:]),
                    reads=[bt("xq_fm")], writes=[bt("dbg")], dma="LD_G")

        def stage_hmm(t):
            for hg in range(8):
                hp = hg % 2
                psh = bt(f"ps_h{hp}")
                builds = [
                    (lambda eng, e_=e_, hg=hg, hp=hp: eng.matmul(
                        ps_h[hp][:], xq_fm[:, e_, :],
                        w1q_sb[:, e_, hg * 512:(hg + 1) * 512],
                        start=(e_ == 0), stop=False))
                    for e_ in range(EC)]
                builds.append(lambda eng, hg=hg, hp=hp: eng.matmul(
                    ps_h[hp][:], ones_row_bf[:],
                    b1_row[0:1, hg * 512:(hg + 1) * 512],
                    start=False, stop=True))
                pl.group("tensor", builds,
                         reads=[bt("xq_fm"), bt("w1q_sb"), bt("b1_row"),
                                bt("ones_row_bf")],
                         writes=[psh])
                pl.op("scalar", lambda eng, hg=hg, hp=hp: eng.activation(
                    out=h_sb[:, hg * 512:(hg + 1) * 512], in_=ps_h[hp][:],
                    func=AF.Gelu), reads=[psh], writes=[bt("h_sb")])

        def stage_rest(t):
            par = t % 2
            xb = bt(f"x2_tile{par}")
            pl.op("vector", lambda eng: eng.reduce_max(
                out=hs_col[:, 3:4], in_=h_sb[:], axis=AX,
                apply_absolute_value=True),
                reads=[bt("h_sb")], writes=[bt("hs_col")])
            pl.op("vector", lambda eng: eng.tensor_scalar_max(
                out=hs_col[:, 3:4], in0=hs_col[:, 3:4], scalar1=EPS),
                reads=[bt("hs_col")], writes=[bt("hs_col")])
            pl.op("vector", lambda eng: eng.reciprocal(
                out=hs_col[:, 4:5], in_=hs_col[:, 3:4]),
                reads=[bt("hs_col")], writes=[bt("hs_col")])
            pl.op("vector", lambda eng: eng.tensor_scalar_mul(
                out=hs_col[:, 4:5], in0=hs_col[:, 4:5], scalar1=QB),
                reads=[bt("hs_col")], writes=[bt("hs_col")])
            pl.op("vector", lambda eng: eng.tensor_scalar_mul(
                out=hs_col[:, 5:6], in0=hs_col[:, 3:4], scalar1=1.0 / QB),
                reads=[bt("hs_col")], writes=[bt("hs_col")])
            op_ = t % 2
            pso = bt(f"ps_o2{op_}")
            for qtr in range(4):
                lo = qtr * E
                pl.op("vector", lambda eng, lo=lo: eng.tensor_scalar(
                    out=f1k[1][:], in0=h_sb[:, lo:lo + E],
                    scalar1=hs_col[:, 4:5], scalar2=None, op0=ALU.mult),
                    reads=[bt("h_sb"), bt("hs_col")], writes=[bt("f1k1")])
                emit_round(f1k[1][:], bt("f1k1"), xqi_t[:], bt("xqi"),
                           f1k[2][:], bt("f1k2"), f1k[0][:], bt("f1k0"),
                           f1k[1][:], bt("f1k1"))
                pl.op("vector", lambda eng: eng.tensor_scalar(
                    out=hq_bf[:], in0=f1k[1][:], scalar1=hs_col[:, 5:6],
                    scalar2=None, op0=ALU.mult),
                    reads=[bt("f1k1"), bt("hs_col")], writes=[bt("hq_bf")])
                tp2 = qtr % 2
                pst2 = bt(f"ps_t3{tp2}")
                pl.group("tensor", [
                    (lambda eng, e_=e_, tp2=tp2: eng.transpose(
                        ps_t3[tp2][:, e_ * P:(e_ + 1) * P],
                        hq_bf[:, e_ * P:(e_ + 1) * P], ident[:]))
                    for e_ in range(EC)],
                    reads=[bt("hq_bf"), bt("ident")], writes=[pst2])
                pl.op("scalar", lambda eng, qtr=qtr, tp2=tp2: eng.copy(
                    out=hq_fm[:, qtr * 8:(qtr + 1) * 8, :],
                    in_=ps_t3[tp2].rearrange("p (c q) -> p c q", c=EC)),
                    reads=[pst2], writes=[bt("hq_fm")])
                builds = []
                for e_ in range(8):
                    hc = qtr * 8 + e_
                    for half in range(2):
                        builds.append(
                            lambda eng, hc=hc, half=half, op_=op_:
                            eng.matmul(
                                ps_o2[op_][:, half * 512:(half + 1) * 512],
                                hq_fm[:, hc, :],
                                w2q_sb[:, hc, half * 512:(half + 1) * 512],
                                start=(hc == 0), stop=False))
                pl.group("tensor", builds,
                         reads=[bt("hq_fm"), bt("w2q_sb")], writes=[pso])
            if t == 0 and _CFG["debug_scratch"]:
                pl.op("gpsimd", lambda eng: eng.dma_start(
                    out=dbg_h[:], in_=h_sb[:]),
                    reads=[bt("h_sb")], writes=[bt("dbg")], dma="LD_G")
                pl.op("gpsimd", lambda eng: eng.dma_start(
                    out=dbg_hqfm[:], in_=hq_fm[:]),
                    reads=[bt("hq_fm")], writes=[bt("dbg")], dma="LD_G")
            pl.group("tensor", [
                (lambda eng, half=half, op_=op_: eng.matmul(
                    ps_o2[op_][:, half * 512:(half + 1) * 512],
                    ones_row_bf[:], b2_row[0:1, half * 512:(half + 1) * 512],
                    start=False, stop=True))
                for half in range(2)],
                reads=[bt("b2_row"), bt("ones_row_bf")], writes=[pso])
            ls4 = ln_scr3[t % 2]
            lsb4 = bt(f"ln_scr3_{t % 2}")
            pl.op("vector", lambda eng, op_=op_, ls=ls4: eng.bn_stats(
                out=ls[:, 0:6], in_=ps_o2[op_][:, 0:512]),
                reads=[pso], writes=[lsb4])
            pl.op("vector", lambda eng, op_=op_, ls=ls4: eng.bn_stats(
                out=ls[:, 6:12], in_=ps_o2[op_][:, 512:1024]),
                reads=[pso], writes=[lsb4])
            pl.op("vector", lambda eng, ls=ls4: eng.bn_aggr(
                out=ls[:, 12:14], in_=ls[:, 0:12]),
                reads=[lsb4], writes=[lsb4])
            pl.op("vector", lambda eng, ls=ls4: eng.tensor_scalar_add(
                out=ls[:, 14:15], in0=ls[:, 13:14], scalar1=EPS),
                reads=[lsb4], writes=[lsb4])
            pl.op("scalar", lambda eng, ls=ls4: eng.activation(
                out=ls[:, 14:15], in_=ls[:, 14:15], func=AF.Sqrt),
                reads=[lsb4], writes=[lsb4])
            pl.op("vector", lambda eng, ls=ls4: eng.reciprocal(
                out=ls[:, 14:15], in_=ls[:, 14:15]),
                reads=[lsb4], writes=[lsb4])
            pl.op("vector", lambda eng, ls=ls4: eng.scalar_tensor_tensor(
                out=ls[:, 15:16], in0=ls[:, 12:13], scalar=-1.0,
                in1=ls[:, 14:15], op0=ALU.mult, op1=ALU.mult),
                reads=[lsb4], writes=[lsb4])
            if t == 0 and _CFG["debug_scratch"]:
                pl.op("gpsimd", lambda eng: eng.dma_start(
                    out=dbg_ls[:], in_=ln_scr3[0][:]),
                    reads=[bt("ln_scr3_0")], writes=[bt("dbg")], dma="LD_G")
                pl.op("vector", lambda eng, op_=op_: eng.tensor_copy(
                    out=f1k[0][:], in_=ps_o2[op_][:]),
                    reads=[pso], writes=[bt("f1k0")])
                pl.op("gpsimd", lambda eng: eng.dma_start(
                    out=dbg_o2[:], in_=f1k[0][:]),
                    reads=[bt("f1k0")], writes=[bt("dbg")], dma="LD_G")
            sp = 0
            ysb = bt("y_stage0")
            pl.op("vector", lambda eng, op_=op_, ls=ls4, sp=sp:
                  eng.tensor_scalar(
                      out=y_stage[sp][:], in0=ps_o2[op_][:],
                      scalar1=ls[:, 14:15], scalar2=ls[:, 15:16],
                      op0=ALU.mult, op1=ALU.add),
                  reads=[pso, lsb4], writes=[ysb])
            pl.op("vector", lambda eng, par=par, sp=sp: eng.tensor_add(
                out=y_stage[sp][:], in0=y_stage[sp][:], in1=x2_tile[par][:]),
                reads=[ysb, xb], writes=[ysb])
            if t + 2 < NT_OWN:
                x2_load(t + 2)
            pl.op("scalar", lambda eng, t=t, sp=sp: eng.dma_start(
                out=y_out[t * P:(t + 1) * P, :], in_=y_stage[sp][:]),
                reads=[ysb], writes=[bt("y_out")], dma="LD_A")

        stage_x(0)
        for t in range(NT_OWN):
            stage_hmm(t)
            if t + 1 < NT_OWN:
                stage_x(t + 1)
            stage_rest(t)

        ps3.close()
        pl.barrier()
        pl.finish(nc, sems)
    return nc


_NC_CACHE = {}


def kernel(L0_attn_x, L0_attn_Wq, L0_attn_bq, L0_attn_Wk, L0_attn_bk,
           L0_attn_Wv, L0_attn_bv, L0_ffn_W1, L0_ffn_b1, L0_ffn_W2, L0_ffn_b2):
    x = np.ascontiguousarray(
        np.asarray(L0_attn_x, dtype=np.float32).reshape(S_TOT, E))
    ins = {
        "wqt": np.ascontiguousarray(np.asarray(L0_attn_Wq, np.float32)),
        "wkt": np.ascontiguousarray(np.asarray(L0_attn_Wk, np.float32)),
        "wvt": np.ascontiguousarray(np.asarray(L0_attn_Wv, np.float32).T),
        "w1t": np.ascontiguousarray(np.asarray(L0_ffn_W1, np.float32).T),
        "w2t": np.ascontiguousarray(np.asarray(L0_ffn_W2, np.float32).T),
        "bq_in": np.ascontiguousarray(np.asarray(L0_attn_bq, np.float32)),
        "bk_in": np.ascontiguousarray(np.asarray(L0_attn_bk, np.float32)),
        "bv_in": np.ascontiguousarray(np.asarray(L0_attn_bv, np.float32)),
        "b1_in": np.ascontiguousarray(np.asarray(L0_ffn_b1, np.float32)),
        "b2_in": np.ascontiguousarray(np.asarray(L0_ffn_b2, np.float32)),
    }
    if "prog" not in _NC_CACHE:
        _NC_CACHE["prog"] = build_program()
    nc = _NC_CACHE["prog"]

    in_maps = []
    for c in range(NCORES):
        m = dict(ins)
        m["x_in"] = np.ascontiguousarray(np.roll(x, -c * QOWN, axis=0))
        in_maps.append(m)
    res = run_bass_kernel_spmd(nc, in_maps, core_ids=list(range(NCORES)))
    out = np.concatenate([res.results[c]["y_out"] for c in range(NCORES)],
                         axis=0)
    return out.reshape(1, S_TOT, E).astype(np.float32)


# version 30
# speedup vs baseline: 1.0335x; 1.0335x over previous
import numpy as np
from contextlib import ExitStack

import concourse.bass as bass
import concourse.mybir as mybir
from concourse.bass_utils import run_bass_kernel_spmd

F32 = mybir.dt.float32
BF16 = mybir.dt.bfloat16
I32 = mybir.dt.int32
AF = mybir.ActivationFunctionType
ALU = mybir.AluOpType
AX = mybir.AxisListType.X

P = 128
E = 1024
H = 4096
S_TOT = 16384
NCORES = 8
QOWN = S_TOT // NCORES
EC = E // P
DC = E // P
HC = H // P
NT_OWN = QOWN // P
NT_ALL = S_TOT // P
NG_ALL = NT_ALL // 4
KG = 1024
NKG = S_TOT // KG
QG = 512
NQG = QOWN // QG
KT_PG = KG // P
C_SHIFT = 40.0
EPS = 1e-5
QB = 127.0

ENGINES = ("sync", "tensor", "vector", "scalar", "gpsimd")
ESEM = {"sync": "SP", "tensor": "PE", "vector": "DVE", "scalar": "ACT",
        "gpsimd": "GP"}
DMA_POOL = {"LD_S": 8, "LD_G": 8, "LD_A": 8}
DMA_SEMS = tuple(f"{q}{i}" for q, r in DMA_POOL.items() for i in range(r))
SEMS = ("SP", "PE", "DVE", "ACT", "GP") + DMA_SEMS

_CFG = {"phases": 3, "debug_scratch": False}


class BufT:
    __slots__ = ("w", "r")

    def __init__(self):
        self.w = {}
        self.r = {}


def _merge(d, sem, n):
    if d.get(sem, 0) < n:
        d[sem] = n


class Plan:
    def __init__(self):
        self.ops = {e: [] for e in ENGINES}
        self.cnt = {s: 0 for s in SEMS}
        self.wm = {e: {} for e in ENGINES}
        self.rr = {q: 0 for q in DMA_POOL}

    def _deps(self, reads, writes, extra):
        deps = {}
        for b in reads:
            for s, c in b.w.items():
                _merge(deps, s, c)
        for b in writes:
            for s, c in b.w.items():
                _merge(deps, s, c)
            for s, c in b.r.items():
                _merge(deps, s, c)
        for s, c in extra:
            _merge(deps, s, c)
        return deps

    def _wait(self, engine, deps):
        wm = self.wm[engine]
        for sem, c in deps.items():
            if c <= 0 or wm.get(sem, 0) >= c:
                continue
            wm[sem] = c
            self.ops[engine].append(
                lambda eng, sems, s=sem, c=c: eng.wait_ge(sems[s], c))

    def _record(self, reads, writes, sem, n):
        for b in reads:
            _merge(b.r, sem, n)
        for b in writes:
            b.w = {sem: n}
            b.r = {}

    def op(self, engine, build, reads=(), writes=(), dma=None, extra=()):
        deps = self._deps(reads, writes, extra)
        if dma is not None:
            r = DMA_POOL[dma]
            sem = f"{dma}{self.rr[dma] % r}"
            self.rr[dma] += 1
            _merge(deps, sem, self.cnt[sem])
            inc = 16
        else:
            sem = ESEM[engine]
            inc = 1
        self._wait(engine, deps)
        self.cnt[sem] += inc
        n = self.cnt[sem]
        self.ops[engine].append(
            lambda eng, sems, build=build, s=sem, i=inc:
            build(eng).then_inc(sems[s], i))
        self._record(reads, writes, sem, n)
        return (sem, n)

    def group(self, engine, builds, reads=(), writes=(), extra=()):
        self._wait(engine, self._deps(reads, writes, extra))
        sem = ESEM[engine]
        for build in builds:
            self.cnt[sem] += 1
            self.ops[engine].append(
                lambda eng, sems, build=build, s=sem:
                build(eng).then_inc(sems[s], 1))
        n = self.cnt[sem]
        self._record(reads, writes, sem, n)
        return (sem, n)

    def barrier(self):
        counts = dict(self.cnt)
        for e in ENGINES:
            self._wait(e, counts)

    def finish(self, nc, sems):
        final = list(self.cnt.items())
        with nc.Block() as block:

            @block.sync
            def _(eng):
                for fn in self.ops["sync"]:
                    fn(eng, sems)
                for (s, c) in final:
                    if c > 0:
                        eng.wait_ge(sems[s], c)

            @block.tensor
            def _(eng):
                for fn in self.ops["tensor"]:
                    fn(eng, sems)

            @block.vector
            def _(eng):
                for fn in self.ops["vector"]:
                    fn(eng, sems)

            @block.scalar
            def _(eng):
                for fn in self.ops["scalar"]:
                    fn(eng, sems)

            @block.gpsimd
            def _(eng):
                for fn in self.ops["gpsimd"]:
                    fn(eng, sems)


def build_program():
    nc = bass.Bass("TRN2", target_bir_lowering=False, debug=False)
    skind = "ExternalOutput" if _CFG["debug_scratch"] else "Internal"

    x_in = nc.dram_tensor("x_in", [S_TOT, E], F32, kind="ExternalInput").ap()
    wqt = nc.dram_tensor("wqt", [E, E], F32, kind="ExternalInput").ap()
    wkt = nc.dram_tensor("wkt", [E, E], F32, kind="ExternalInput").ap()
    wvt = nc.dram_tensor("wvt", [E, E], F32, kind="ExternalInput").ap()
    w1t = nc.dram_tensor("w1t", [E, H], F32, kind="ExternalInput").ap()
    w2t = nc.dram_tensor("w2t", [H, E], F32, kind="ExternalInput").ap()
    bq_in = nc.dram_tensor("bq_in", [E], F32, kind="ExternalInput").ap()
    bk_in = nc.dram_tensor("bk_in", [E], F32, kind="ExternalInput").ap()
    bv_in = nc.dram_tensor("bv_in", [E], F32, kind="ExternalInput").ap()
    b1_in = nc.dram_tensor("b1_in", [H], F32, kind="ExternalInput").ap()
    b2_in = nc.dram_tensor("b2_in", [E], F32, kind="ExternalInput").ap()
    y_out = nc.dram_tensor("y_out", [QOWN, E], F32, kind="ExternalOutput").ap()

    xn_scr = nc.dram_tensor("xn_scr", [NT_ALL, P, E], BF16, kind="Internal").ap()
    x2_scr = nc.dram_tensor("x2_scr", [NT_OWN, P, E], F32, kind=skind).ap()
    w1q_scr = nc.dram_tensor("w1q_scr", [EC, P, H], BF16, kind=skind).ap()
    w2q_scr = nc.dram_tensor("w2q_scr", [HC, P, E], BF16, kind=skind).ap()
    l_scr = nc.dram_tensor("l_scr", [QOWN], F32, kind=skind).ap()
    dbg_xq = nc.dram_tensor("dbg_xq", [P, E], BF16, kind=skind).ap()
    dbg_h = nc.dram_tensor("dbg_h", [P, H], F32, kind=skind).ap()
    dbg_hqfm = nc.dram_tensor("dbg_hqfm", [P, HC, P], BF16, kind=skind).ap()
    dbg_xqfm = nc.dram_tensor("dbg_xqfm", [P, EC, P], BF16, kind=skind).ap()
    dbg_ls = nc.dram_tensor("dbg_ls", [P, 16], F32, kind=skind).ap()
    dbg_o2 = nc.dram_tensor("dbg_o2", [P, E], F32, kind=skind).ap()

    pl = Plan()
    B = {}

    def bt(name):
        if name not in B:
            B[name] = BufT()
        return B[name]

    with ExitStack() as ctx:
        _probe = nc.alloc_sbuf_tensor("_base_probe", [P, 8], F32)
        SB_BASE = None
        for _a in nc.m.functions[0].allocations:
            try:
                if _a.memorylocations[0].name == "_base_probe":
                    SB_BASE = int(_a.memorylocations[0].addr)
            except Exception:
                pass
        assert SB_BASE is not None and SB_BASE >= 4096, SB_BASE
        SB_LIMIT = SB_BASE + 212832
        cursor = [SB_BASE + 64]

        def at(name, shape, dtype, offset=None):
            if offset is not None:
                cursor[0] = offset
            off = (cursor[0] + 63) // 64 * 64
            nbytes = int(np.prod(shape[1:])) * mybir.dt.size(dtype)
            t = nc.alloc_sbuf_tensor_at(name, list(shape), dtype, offset=off)
            cursor[0] = off + nbytes
            return t.ap()

        ident = at("ident", [P, P], BF16)
        ones_row_bf = at("ones_row_bf", [1, P], BF16)
        p3_persist_end = cursor[0]
        ones_col = at("ones_col", [P, 1], BF16)
        ones_row_f = at("ones_row_f", [1, P], F32)
        ones_f32col = at("ones_f32col", [P, 1], F32)
        c_nshift = at("c_nshift", [P, 1], F32)
        ones_row5 = at("ones_row5", [1, 512], BF16)
        bq_row = at("bq_row", [1, E], BF16)
        bk_row = at("bk_row", [1, E], BF16)
        bv_bcast = at("bv_bcast", [P, E], BF16)
        linv_col = at("linv_col", [P, NT_OWN], F32)
        wsb = at("wsb", [P, 8], F32)
        ws_row = at("ws_row", [1, 8], F32)
        persist_end = cursor[0]

        q_fm = at("q_fm", [P, DC, QOWN], BF16)
        p1_base = cursor[0]

        wqt_sb = at("wqt_sb", [P, EC, E], BF16, offset=p1_base)
        wkt_sb = at("wkt_sb", [P, EC, E], BF16)
        x_tile = [at(f"x_tile{i}", [P, E], F32) for i in range(4)]
        xn_tm = [at(f"xn_tm{i}", [P, E], BF16) for i in range(3)]
        xn_fm = [at(f"xn_fm{i}", [P, EC, 512], BF16) for i in range(4)]
        k_stage = [at(f"k_stage{i}", [P, E], BF16) for i in range(4)]
        v_stage = [at(f"v_stage{i}", [P, E], BF16) for i in range(4)]
        ln_scr1 = [at(f"ln_scr1_{i}", [P, 16], F32) for i in range(4)]
        wq_stream = [at(f"wq_stream{i}", [P, H], F32) for i in range(3)]
        wq_sq = at("wq_sq", [P, 2048], F32)
        wq_out = at("wq_out", [P, H], BF16)
        asums = at("asums", [P, EC + HC + 2], F32)
        bq_col = at("bq_col", [P, EC], BF16)
        p1_end = cursor[0]

        z_acc = at("z_acc", [P, NT_OWN, E], F32, offset=p1_base)
        k_buf = [at(f"k_buf{i}", [P, DC, KG], BF16) for i in range(2)]
        v_buf = [at(f"v_buf{i}", [P, KT_PG, E], BF16) for i in range(2)]
        p_sb = at("p_sb", [P, KT_PG, QG], BF16)
        l_acc = at("l_acc", [1, QOWN], F32)
        xr_tile = [at(f"xr_tile{i}", [P, E], F32) for i in range(2)]
        ep_znbf = at("ep_znbf", [P, E], BF16)
        ep_znfm = [at("ep_znfm0", [P, EC, P], BF16)]
        x2_stage = [at(f"x2_stage{i}", [P, E], F32) for i in range(2)]
        ln_scr2 = [at(f"ln_scr2_{i}", [P, 16], F32) for i in range(2)]
        ep_znfm.append(at("ep_znfm1", [P, EC, P], BF16))
        p2_end = cursor[0]
        wv_ep = at("wv_ep", [P, EC, E], BF16, offset=p1_base + 98304)
        assert cursor[0] <= p1_base + 114688, cursor[0]

        w1q_sb = at("w1q_sb", [P, EC, H], BF16, offset=persist_end)
        w2q_sb = at("w2q_sb", [P, HC, E], BF16, offset=p1_base + 65536)
        b1_row = at("b1_row", [1, H], BF16)
        b2_row = at("b2_row", [1, E], BF16)
        cursor[0] = p1_base + 32768
        x2_tile = [at(f"x2_tile{i}", [P, E], F32) for i in range(2)]
        f1k = [at(f"f1k{i}", [P, E], F32) for i in range(3)]
        xqi_t = at("xqi_t", [P, E], I32)
        ln_scr3 = [at(f"ln_scr3_{i}", [P, 16], F32) for i in range(2)]
        hs_col = at("hs_col", [P, 8], F32)
        y_stage = [at("y_stage0", [P, E], F32)] * 2
        hq_bf = at("hq_bf", [P, E], BF16)
        thr_col = at("thr_col", [P, 4], F32)
        p3_endB = cursor[0]
        assert p3_endB <= p1_base + 65536, p3_endB
        cursor[0] = p1_base + 65536 + 65536 + 10240
        h_sb = at("h_sb", [P, H], F32)
        hq_fm = at("hq_fm", [P, HC, P], BF16)
        xq_bf = at("xq_bf", [P, E], BF16)
        xq_fm = at("xq_fm", [P, EC, P], BF16)
        p3_end = cursor[0]

        assert max(p1_end, p2_end, p3_end) <= SB_LIMIT, \
            (SB_BASE, p1_end, p2_end, p3_end)

        sems_ctx = ctx.enter_context(ExitStack())
        sems = {s: sems_ctx.enter_context(nc.semaphore(s)) for s in SEMS}

        def ms(ap_, val):
            return lambda eng: eng.memset(ap_, val)

        pl.op("gpsimd", ms(ident[:], 0.0), writes=[bt("ident")])
        pl.op("gpsimd", lambda eng: eng.affine_select(
            out=ident[:], in_=ident[:], compare_op=ALU.not_equal, fill=1.0,
            base=0, pattern=[[-1, P]], channel_multiplier=1),
            reads=[bt("ident")], writes=[bt("ident")])
        pl.op("gpsimd", ms(ones_col[:], 1.0), writes=[bt("ones_col")])
        pl.op("gpsimd", ms(ones_row_bf[:], 1.0), writes=[bt("ones_row_bf")])
        pl.op("gpsimd", ms(ones_row_f[:], 1.0), writes=[bt("ones_row_f")])
        pl.op("gpsimd", ms(ones_f32col[:], 1.0), writes=[bt("ones_f32col")])
        pl.op("gpsimd", ms(c_nshift[:], -C_SHIFT), writes=[bt("c_nshift")])

        pl.op("gpsimd", ms(ones_row5[:], 1.0), writes=[bt("ones_row5")])
        for ec in range(EC):
            pl.op("gpsimd", lambda eng, ec=ec: eng.dma_start(
                out=bq_col[:, ec:ec + 1],
                in_=bq_in[ec * P:(ec + 1) * P].rearrange("(p o) -> p o", o=1)),
                writes=[bt("bq_col")], dma="LD_G")
        pl.op("gpsimd", lambda eng: eng.dma_start(
            out=bv_bcast[0:1, :], in_=bv_in[:]),
            writes=[bt("bv_bcast")], dma="LD_G")
        for (src, dst, nm) in ((wkt, wkt_sb, "wkt_sb"),
                               (wqt, wqt_sb, "wqt_sb")):
            for ec in range(EC):
                pl.op("gpsimd", lambda eng, s=src, d=dst, ec=ec:
                      eng.dma_start(out=d[:, ec, :],
                                    in_=s[ec * P:(ec + 1) * P, :]),
                      writes=[bt(nm)], dma="LD_G")

        ps1 = ExitStack()
        ps_k = [ps1.enter_context(nc.psum_tensor(f"ps_k{i}", [P, E], F32))
                for i in range(2)]

        m_chunks = k_stage + v_stage

        def m_name(e_):
            return f"k_stage{e_}" if e_ < 4 else f"v_stage{e_ - 4}"

        for half in range(2):
            pl.group("tensor", [
                (lambda eng, sc=sc, half=half: eng.matmul(
                    ps_k[0][0:1, half * 512:(half + 1) * 512],
                    bq_col[:, sc:sc + 1],
                    wkt_sb[:, sc, half * 512:(half + 1) * 512],
                    start=(sc == 0), stop=(sc == EC - 1)))
                for sc in range(EC)],
                reads=[bt("bq_col"), bt("wkt_sb")], writes=[bt("ps_k0")])
        pl.op("vector", lambda eng: eng.tensor_copy(
            out=bq_row[0:1, :], in_=ps_k[0][0:1, :]),
            reads=[bt("ps_k0")], writes=[bt("bq_row")])
        for ec in range(EC):
            pp = ec % 2
            psb = bt(f"ps_k{pp}")
            builds = []
            for half in range(2):
                for sc in range(EC):
                    builds.append(
                        lambda eng, sc=sc, ec=ec, half=half, pp=pp:
                        eng.matmul(
                            ps_k[pp][:, half * 512:(half + 1) * 512],
                            wqt_sb[:, sc, ec * P:(ec + 1) * P],
                            wkt_sb[:, sc, half * 512:(half + 1) * 512],
                            start=(sc == 0), stop=(sc == EC - 1)))
            pl.group("tensor", builds,
                     reads=[bt("wqt_sb"), bt("wkt_sb")], writes=[psb])
            pl.op("scalar", lambda eng, ec=ec, pp=pp: eng.copy(
                out=m_chunks[ec][:], in_=ps_k[pp][:]),
                reads=[psb], writes=[bt(m_name(ec))])

        wq_tiles = [("w1", i) for i in range(EC)] + \
                   [("w2", i) for i in range(HC)]
        wq_a_idx = [0]

        wq_a_loaded = [0]

        def wq_load(idx):
            if idx >= len(wq_tiles) or idx < wq_a_loaded[0]:
                return
            wq_a_loaded[0] = idx + 1
            which, i = wq_tiles[idx]
            par = idx % 3
            src_t = w1t if which == "w1" else w2t
            cols = H if which == "w1" else E
            pl.op("gpsimd", lambda eng, s=src_t, i=i, par=par, cols=cols:
                  eng.dma_start(out=wq_stream[par][:, 0:cols],
                                in_=s[i * P:(i + 1) * P, :]),
                  writes=[bt(f"wq_stream{par}")], dma="LD_G")

        def emit_wquant_a(count):
            for _ in range(count):
                if wq_a_idx[0] >= len(wq_tiles):
                    return
                idx = wq_a_idx[0]
                wq_a_idx[0] += 1
                wq_load(idx)
                wq_load(idx + 1)
                wq_load(idx + 2)
                which, i = wq_tiles[idx]
                par = idx % 3
                cols = H if which == "w1" else E
                sbuf = bt(f"wq_stream{par}")
                pl.op("vector", lambda eng, par=par, cols=cols, col=idx:
                      eng.reduce_sum(out=asums[:, col:col + 1],
                                     in_=wq_stream[par][:, 0:cols], axis=AX,
                                     apply_absolute_value=True),
                      reads=[sbuf], writes=[bt("asums")])

        def emit_wquant_combine():
            pl.op("vector", lambda eng: eng.reduce_sum(
                out=asums[:, EC + HC:EC + HC + 1], in_=asums[:, 0:EC],
                axis=AX), reads=[bt("asums")], writes=[bt("asums")])
            pl.op("vector", lambda eng: eng.reduce_sum(
                out=asums[:, EC + HC + 1:EC + HC + 2],
                in_=asums[:, EC:EC + HC], axis=AX),
                reads=[bt("asums")], writes=[bt("asums")])
            pl.group("tensor", [
                lambda eng: eng.matmul(
                    ps_k[0][0:1, 0:1], asums[:, EC + HC:EC + HC + 1],
                    ones_f32col[:], start=True, stop=True),
                lambda eng: eng.matmul(
                    ps_k[0][0:1, 1:2], asums[:, EC + HC + 1:EC + HC + 2],
                    ones_f32col[:], start=True, stop=True)],
                reads=[bt("asums"), bt("ones_f32col")], writes=[bt("ps_k0")])
            pl.op("vector", lambda eng: eng.tensor_scalar(
                out=ws_row[0:1, 0:1], in0=ps_k[0][0:1, 0:1],
                scalar1=1.0 / (H * E), scalar2=EPS, op0=ALU.mult,
                op1=ALU.max),
                reads=[bt("ps_k0")], writes=[bt("ws_row")])
            pl.op("vector", lambda eng: eng.tensor_scalar(
                out=ws_row[0:1, 3:4], in0=ps_k[0][0:1, 1:2],
                scalar1=1.0 / (H * E), scalar2=EPS, op0=ALU.mult,
                op1=ALU.max),
                reads=[bt("ps_k0")], writes=[bt("ws_row")])
            for (src_c, dst_c) in ((0, 1), (3, 4)):
                pl.op("vector", lambda eng, s=src_c, d=dst_c:
                      eng.tensor_single_scalar(
                          out=ws_row[0:1, d:d + 1], in_=ws_row[0:1, s:s + 1],
                          scalar=0.5, op=ALU.mult),
                      reads=[bt("ws_row")], writes=[bt("ws_row")])
            for (src_c, dst_c) in ((1, 2), (4, 5), (0, 6), (3, 7)):
                pl.op("vector", lambda eng, s=src_c, d=dst_c:
                      eng.tensor_single_scalar(
                          out=ws_row[0:1, d:d + 1], in_=ws_row[0:1, s:s + 1],
                          scalar=-1.0, op=ALU.mult),
                      reads=[bt("ws_row")], writes=[bt("ws_row")])
            pl.group("tensor", [lambda eng: eng.matmul(
                ps_k[0][:, 0:8], ones_row_f[:], ws_row[:], start=True,
                stop=True)],
                reads=[bt("ws_row"), bt("ones_row_f")], writes=[bt("ps_k0")])
            pl.op("vector", lambda eng: eng.tensor_copy(
                out=wsb[:], in_=ps_k[0][:, 0:8]),
                reads=[bt("ps_k0")], writes=[bt("wsb")])

        wq_b_idx = [0]

        wq_b_loaded = [0]

        def wq_load_b(idx):
            if idx >= len(wq_tiles) or idx < wq_b_loaded[0]:
                return
            wq_b_loaded[0] = idx + 1
            which, i = wq_tiles[idx]
            par = idx % 3
            src_t = w1t if which == "w1" else w2t
            cols = H if which == "w1" else E
            pl.op("gpsimd", lambda eng, s=src_t, i=i, par=par, cols=cols:
                  eng.dma_start(out=wq_stream[par][:, 0:cols],
                                in_=s[i * P:(i + 1) * P, :]),
                  writes=[bt(f"wq_stream{par}")], dma="LD_G")

        def emit_wquant_b(count):
            for _ in range(count):
                if wq_b_idx[0] >= len(wq_tiles):
                    return
                idx = wq_b_idx[0]
                which, i = wq_tiles[idx]
                wq_b_idx[0] += 1
                wq_load_b(idx)
                wq_load_b(idx + 1)
                wq_load_b(idx + 2)
                par = idx % 3
                dst = w1q_scr if which == "w1" else w2q_scr
                dstb = bt("w1q_scr") if which == "w1" else bt("w2q_scr")
                cols = H if which == "w1" else E
                o = 0 if which == "w1" else 3
                sbuf = bt(f"wq_stream{par}")
                nws_c = 6 if which == "w1" else 7
                for hh in range(2 if cols == H else 1):
                    lo = hh * 2048
                    w = min(cols, lo + 2048) - lo
                    pl.op("vector", lambda eng, par=par, lo=lo, w=w, o=o:
                          eng.tensor_scalar(
                              out=wq_sq[:, 0:w],
                              in0=wq_stream[par][:, lo:lo + w],
                              scalar1=wsb[:, o + 1:o + 2], scalar2=None,
                              op0=ALU.is_ge),
                          reads=[sbuf, bt("wsb")], writes=[bt("wq_sq")])
                    pl.op("vector", lambda eng, par=par, lo=lo, w=w, o=o:
                          eng.scalar_tensor_tensor(
                              out=wq_sq[:, 0:w],
                              in0=wq_stream[par][:, lo:lo + w],
                              scalar=wsb[:, o + 2:o + 3], in1=wq_sq[:, 0:w],
                              op0=ALU.is_le, op1=ALU.subtract),
                          reads=[sbuf, bt("wsb"), bt("wq_sq")],
                          writes=[bt("wq_sq")])
                    pl.op("vector", lambda eng, lo=lo, w=w, nws_c=nws_c:
                          eng.tensor_scalar(
                              out=wq_out[:, lo:lo + w], in0=wq_sq[:, 0:w],
                              scalar1=wsb[:, nws_c:nws_c + 1], scalar2=None,
                              op0=ALU.mult),
                          reads=[bt("wq_sq"), bt("wsb")],
                          writes=[bt("wq_out")])
                pl.op("scalar", lambda eng, d=dst, i=i, cols=cols:
                      eng.dma_start(out=d[i], in_=wq_out[:, 0:cols]),
                      reads=[bt("wq_out")], writes=[dstb], dma="LD_A")

        def x_load(t):
            par = t % 4
            pl.op("scalar", lambda eng, t=t, par=par: eng.dma_start(
                out=x_tile[par][:], in_=x_in[t * P:(t + 1) * P, :]),
                writes=[bt(f"x_tile{par}")], dma="LD_A")

        def xn_load(g):
            xp = g % 4
            xfm = bt(f"xn_fm{xp}")
            srcs = [bt(f"xn_scr_t{g * 4 + i}") for i in range(4)]
            for e_ in range(EC):
                src_ap = xn_scr[g * 4:(g + 1) * 4, :,
                                e_ * P:(e_ + 1) * P].rearrange(
                                    "a p d -> (a p) d")
                pl.op("sync", lambda eng, e_=e_, xp=xp, src_ap=src_ap:
                      eng.dma_start_transpose(xn_fm[xp][:, e_, :], src_ap),
                      reads=srcs, writes=[xfm], dma="LD_S")

        def emit_ln(eng_scr, scr_b, src_ap, src_b):
            pl.op("vector", lambda eng: eng.bn_stats(
                out=eng_scr[:, 0:6], in_=src_ap[:, 0:512]),
                reads=[src_b], writes=[scr_b])
            pl.op("vector", lambda eng: eng.bn_stats(
                out=eng_scr[:, 6:12], in_=src_ap[:, 512:1024]),
                reads=[src_b], writes=[scr_b])
            pl.op("vector", lambda eng: eng.bn_aggr(
                out=eng_scr[:, 12:14], in_=eng_scr[:, 0:12]),
                reads=[scr_b], writes=[scr_b])
            pl.op("vector", lambda eng: eng.tensor_scalar_add(
                out=eng_scr[:, 14:15], in0=eng_scr[:, 13:14], scalar1=EPS),
                reads=[scr_b], writes=[scr_b])
            pl.op("scalar", lambda eng: eng.activation(
                out=eng_scr[:, 14:15], in_=eng_scr[:, 14:15], func=AF.Sqrt),
                reads=[scr_b], writes=[scr_b])
            pl.op("vector", lambda eng: eng.reciprocal(
                out=eng_scr[:, 14:15], in_=eng_scr[:, 14:15]),
                reads=[scr_b], writes=[scr_b])
            pl.op("vector", lambda eng: eng.scalar_tensor_tensor(
                out=eng_scr[:, 15:16], in0=eng_scr[:, 12:13], scalar=-1.0,
                in1=eng_scr[:, 14:15], op0=ALU.mult, op1=ALU.mult),
                reads=[scr_b], writes=[scr_b])

        for t in range(4):
            x_load(t)

        def ln_stats(t):
            ls = ln_scr1[t % 4]
            lsb = bt(f"ln_scr1_{t % 4}")
            xb = bt(f"x_tile{t % 4}")
            pl.op("vector", lambda eng, ls=ls, t=t: eng.bn_stats(
                out=ls[:, 0:6], in_=x_tile[t % 4][:, 0:512]),
                reads=[xb], writes=[lsb])
            pl.op("vector", lambda eng, ls=ls, t=t: eng.bn_stats(
                out=ls[:, 6:12], in_=x_tile[t % 4][:, 512:1024]),
                reads=[xb], writes=[lsb])
            pl.op("vector", lambda eng, ls=ls: eng.bn_aggr(
                out=ls[:, 12:14], in_=ls[:, 0:12]),
                reads=[lsb], writes=[lsb])
            pl.op("vector", lambda eng, ls=ls: eng.tensor_scalar_add(
                out=ls[:, 14:15], in0=ls[:, 13:14], scalar1=EPS),
                reads=[lsb], writes=[lsb])
            pl.op("scalar", lambda eng, ls=ls: eng.activation(
                out=ls[:, 14:15], in_=ls[:, 14:15], func=AF.Sqrt),
                reads=[lsb], writes=[lsb])

        def ln_norm(t):
            xpar = t % 4
            par = t % 3
            ls = ln_scr1[t % 4]
            lsb = bt(f"ln_scr1_{t % 4}")
            xb = bt(f"x_tile{xpar}")
            pl.op("vector", lambda eng, ls=ls: eng.reciprocal(
                out=ls[:, 14:15], in_=ls[:, 14:15]),
                reads=[lsb], writes=[lsb])
            pl.op("vector", lambda eng, ls=ls: eng.scalar_tensor_tensor(
                out=ls[:, 15:16], in0=ls[:, 12:13], scalar=-1.0,
                in1=ls[:, 14:15], op0=ALU.mult, op1=ALU.mult),
                reads=[lsb], writes=[lsb])
            pl.op("vector", lambda eng, xpar=xpar, par=par, ls=ls:
                  eng.tensor_scalar(
                out=xn_tm[par][:], in0=x_tile[xpar][:],
                scalar1=ls[:, 14:15], scalar2=ls[:, 15:16],
                op0=ALU.mult, op1=ALU.add),
                reads=[xb, lsb], writes=[bt(f"xn_tm{par}")])
            if t + 4 < NT_ALL:
                x_load(t + 4)
            pl.op("scalar", lambda eng, t=t, par=par: eng.dma_start(
                out=xn_scr[t], in_=xn_tm[par][:]),
                reads=[bt(f"xn_tm{par}")], writes=[bt(f"xn_scr_t{t}")],
                dma="LD_A")
            if t >= 8 and t % 2 == 1:
                emit_wquant_a(1)
            if t == 105:
                emit_wquant_a(100)
                emit_wquant_combine()
            if t >= 107 and t % 2 == 1:
                emit_wquant_b(1)
            if t % 4 == 3 and t // 4 < 4:
                xn_load(t // 4)

        ln_stats(0)
        for t in range(NT_ALL):
            if t + 1 < NT_ALL:
                ln_stats(t + 1)
            ln_norm(t)

        for g in range(NT_OWN // 4):
            xp = g % 4
            xfm = bt(f"xn_fm{xp}")
            for dc in range(DC):
                psb = bt("ps_k0")
                builds = [
                    (lambda eng, e_=e_, dc=dc, xp=xp:
                     eng.matmul(ps_k[0][:, 0:512],
                                m_chunks[e_][:, dc * P:(dc + 1) * P],
                                xn_fm[xp][:, e_, :],
                                start=(e_ == 0), stop=False))
                    for e_ in range(EC)]
                builds.append(
                    lambda eng, dc=dc: eng.matmul(
                        ps_k[0][:, 0:512],
                        bq_row[0:1, dc * P:(dc + 1) * P],
                        ones_row5[:], start=False, stop=True))
                pl.group("tensor", builds,
                         reads=[bt(m_name(e_)) for e_ in range(EC)] +
                               [xfm, bt("bq_row"), bt("ones_row5")],
                         writes=[psb])
                pl.op("scalar", lambda eng, dc=dc, g=g:
                      eng.copy(
                          out=q_fm[:, dc, g * 512:(g + 1) * 512],
                          in_=ps_k[0][:, 0:512]),
                      reads=[psb], writes=[bt("q_fm")])
                emit_wquant_b(2)
        emit_wquant_b(100)

        ps1.close()
        if _CFG["phases"] == 1:
            pl.barrier()
            pl.finish(nc, sems)
            return nc

        pl.barrier()

        ps2 = ExitStack()
        ps_s = [ps2.enter_context(nc.psum_tensor(f"ps_s{i}", [P, QG], F32))
                for i in range(2)]
        ps_z = [ps2.enter_context(nc.psum_tensor(f"ps_z{i}", [P, 512], F32))
                for i in range(4)]
        ps_l = ps2.enter_context(nc.psum_tensor("ps_l", [1, QG], F32))

        def load_kgroup(kg):
            par = kg % 2
            kb, vb = bt(f"k_buf{par}"), bt(f"v_buf{par}")
            srcs = [bt(f"xn_scr_t{kg * KT_PG + i}") for i in range(KT_PG)]
            for dc in range(DC):
                src_ap = xn_scr[kg * KT_PG:(kg + 1) * KT_PG, :,
                                dc * P:(dc + 1) * P].rearrange(
                                    "a p d -> (a p) d")
                pl.op("sync", lambda eng, dc=dc, par=par, src_ap=src_ap:
                      eng.dma_start_transpose(k_buf[par][:, dc, :], src_ap),
                      reads=srcs, writes=[kb], dma="LD_S")
            for kt in range(KT_PG):
                t = kg * KT_PG + kt
                pl.op("sync", lambda eng, kt=kt, t=t, par=par:
                      eng.dma_start(out=v_buf[par][:, kt, :], in_=xn_scr[t]),
                      reads=[bt(f"xn_scr_t{t}")], writes=[vb], dma="LD_S")

        load_kgroup(0)
        load_kgroup(1)

        for kg in range(NKG):
            par = kg % 2
            kb, vb = bt(f"k_buf{par}"), bt(f"v_buf{par}")
            if kg == NKG - 1:
                wv_gate = list(dict(pl.cnt).items())
                for ec in range(EC):
                    pl.op("gpsimd", lambda eng, ec=ec: eng.dma_start(
                        out=wv_ep[:, ec, :], in_=wvt[ec * P:(ec + 1) * P, :]),
                        writes=[bt("wv_ep")], dma="LD_G", extra=wv_gate)
            for qg in range(NQG):
                for kt in range(KT_PG):
                    sp = kt % 2
                    pss = bt(f"ps_s{sp}")
                    pl.group("tensor", [
                        (lambda eng, dc=dc, kt=kt, sp=sp, par=par, qg=qg:
                         eng.matmul(ps_s[sp][:],
                                    k_buf[par][:, dc, kt * P:(kt + 1) * P],
                                    q_fm[:, dc, qg * QG:(qg + 1) * QG],
                                    start=(dc == 0), stop=(dc == DC - 1)))
                        for dc in range(DC)],
                        reads=[kb, bt("q_fm")], writes=[pss])
                    pl.op("scalar", lambda eng, kt=kt, sp=sp: eng.activation(
                        out=p_sb[:, kt, :], in_=ps_s[sp][:], func=AF.Exp,
                        scale=1.0 / 32.0, bias=c_nshift[:]),
                        reads=[pss, bt("c_nshift")], writes=[bt("p_sb")])
                for qt in range(4):
                    psz = bt(f"ps_z{qt}")
                    pl.group("tensor", [
                        (lambda eng, kt=kt, qt=qt, par=par:
                         eng.matmul(ps_z[qt][:],
                                    p_sb[:, kt, qt * P:(qt + 1) * P],
                                    v_buf[par][:, kt, 0:512],
                                    start=(kt == 0), stop=(kt == KT_PG - 1)))
                        for kt in range(KT_PG)],
                        reads=[bt("p_sb"), vb], writes=[psz])
                pl.group("tensor", [
                    (lambda eng, kt=kt: eng.matmul(
                        ps_l[:], ones_col[:], p_sb[:, kt, :],
                        start=(kt == 0), stop=(kt == KT_PG - 1)))
                    for kt in range(KT_PG)],
                    reads=[bt("p_sb"), bt("ones_col")], writes=[bt("ps_l")])
                for qt in range(4):
                    psz = bt(f"ps_z{qt}")
                    t_own = qg * 4 + qt
                    if kg == 0:
                        pl.op("vector", lambda eng, qt=qt, t=t_own:
                              eng.tensor_copy(out=z_acc[:, t, 0:512],
                                              in_=ps_z[qt][:]),
                              reads=[psz], writes=[bt(f"z_acc{t_own}")])
                    else:
                        pl.op("vector", lambda eng, qt=qt, t=t_own:
                              eng.tensor_add(out=z_acc[:, t, 0:512],
                                             in0=ps_z[qt][:],
                                             in1=z_acc[:, t, 0:512]),
                              reads=[psz, bt(f"z_acc{t_own}")],
                              writes=[bt(f"z_acc{t_own}")])
                if kg == 0:
                    pl.op("vector", lambda eng, qg=qg: eng.tensor_copy(
                        out=l_acc[0:1, qg * QG:(qg + 1) * QG], in_=ps_l[:]),
                        reads=[bt("ps_l")], writes=[bt(f"l_acc{qg}")])
                else:
                    pl.op("vector", lambda eng, qg=qg: eng.tensor_add(
                        out=l_acc[0:1, qg * QG:(qg + 1) * QG], in0=ps_l[:],
                        in1=l_acc[0:1, qg * QG:(qg + 1) * QG]),
                        reads=[bt("ps_l"), bt(f"l_acc{qg}")],
                        writes=[bt(f"l_acc{qg}")])
                for qt in range(4):
                    psz = bt(f"ps_z{qt}")
                    t_own = qg * 4 + qt
                    pl.group("tensor", [
                        (lambda eng, kt=kt, qt=qt, par=par:
                         eng.matmul(ps_z[qt][:],
                                    p_sb[:, kt, qt * P:(qt + 1) * P],
                                    v_buf[par][:, kt, 512:1024],
                                    start=(kt == 0), stop=(kt == KT_PG - 1)))
                        for kt in range(KT_PG)],
                        reads=[bt("p_sb"), vb], writes=[psz])
                for qt in range(4):
                    psz = bt(f"ps_z{qt}")
                    t_own = qg * 4 + qt
                    if kg == 0:
                        pl.op("vector", lambda eng, qt=qt, t=t_own:
                              eng.tensor_copy(out=z_acc[:, t, 512:1024],
                                              in_=ps_z[qt][:]),
                              reads=[psz], writes=[bt(f"z_acc{t_own}")])
                    else:
                        pl.op("vector", lambda eng, qt=qt, t=t_own:
                              eng.tensor_add(out=z_acc[:, t, 512:1024],
                                             in0=ps_z[qt][:],
                                             in1=z_acc[:, t, 512:1024]),
                              reads=[psz, bt(f"z_acc{t_own}")],
                              writes=[bt(f"z_acc{t_own}")])
            if kg + 2 < NKG:
                load_kgroup(kg + 2)

        att_end = dict(pl.cnt)
        zn_gates = {}
        la = [bt(f"l_acc{qg}") for qg in range(NQG)]
        pl.op("vector", lambda eng: eng.reciprocal(out=l_acc[:], in_=l_acc[:]),
              reads=la, writes=la)
        pl.op("gpsimd", lambda eng: eng.dma_start(
            out=l_scr.rearrange("(o q) -> o q", o=1), in_=l_acc[0:1, :]),
            reads=la, writes=[bt("l_scr")], dma="LD_G")
        for tt in range(NT_OWN):
            pl.op("gpsimd", lambda eng, tt=tt: eng.dma_start(
                out=linv_col[:, tt:tt + 1], in_=l_scr[tt * P:(tt + 1) * P]),
                reads=[bt("l_scr")], writes=[bt("linv_col")], dma="LD_G")

        ps2.close()
        ps2e = ExitStack()
        ps_tz = [ps2e.enter_context(nc.psum_tensor(f"ps_tz{i}", [P, E], BF16))
                 for i in range(2)]
        ps_pj = [ps2e.enter_context(nc.psum_tensor(f"ps_pj{i}", [P, E], F32))
                 for i in range(2)]
        att_gateE = list(att_end.items())

        def xr_load(t):
            par = t % 2
            pl.op("scalar", lambda eng, t=t, par=par: eng.dma_start(
                out=xr_tile[par][:], in_=x_in[t * P:(t + 1) * P, :]),
                writes=[bt(f"xr_tile{par}")], dma="LD_A")

        xr_load(0)
        xr_load(1)

        def ep_zn(t):
            par = t % 2
            pl.op("vector", lambda eng, t=t: eng.tensor_scalar(
                out=ep_znbf[:], in0=z_acc[:, t, :],
                scalar1=linv_col[:, t:t + 1], scalar2=None, op0=ALU.mult),
                reads=[bt(f"z_acc{t}"), bt("linv_col")],
                writes=[bt("ep_znbf")],
                extra=(att_gateE if t == 0 else ()))
            zn_gates[t] = ("DVE", pl.cnt["DVE"])
            pl.group("tensor", [
                (lambda eng, e_=e_, par=par: eng.transpose(
                    ps_tz[par][:, e_ * P:(e_ + 1) * P],
                    ep_znbf[:, e_ * P:(e_ + 1) * P], ident[:]))
                for e_ in range(EC)],
                reads=[bt("ep_znbf"), bt("ident")],
                writes=[bt(f"ps_tz{par}")],
                extra=(att_gateE if t < 2 else ()))
            pl.op("vector", lambda eng, par=par: eng.tensor_copy(
                out=ep_znfm[par][:],
                in_=ps_tz[par].rearrange("p (c q) -> p c q", c=EC)),
                reads=[bt(f"ps_tz{par}")], writes=[bt(f"ep_znfm{par}")],
                extra=(att_gateE if t < 2 else ()))

        def ep_proj(t):
            par = t % 2
            builds = []
            for half in range(2):
                for e_ in range(EC):
                    builds.append(
                        lambda eng, e_=e_, half=half, par=par: eng.matmul(
                            ps_pj[par][:, half * 512:(half + 1) * 512],
                            ep_znfm[par][:, e_, :],
                            wv_ep[:, e_, half * 512:(half + 1) * 512],
                            start=(e_ == 0), stop=False))
                builds.append(
                    lambda eng, half=half, par=par: eng.matmul(
                        ps_pj[par][:, half * 512:(half + 1) * 512],
                        ones_row_bf[:],
                        bv_bcast[0:1, half * 512:(half + 1) * 512],
                        start=False, stop=True))
            pl.group("tensor", builds,
                     reads=[bt(f"ep_znfm{par}"), bt("wv_ep"),
                            bt("ones_row_bf"), bt("bv_bcast")],
                     writes=[bt(f"ps_pj{par}")],
                     extra=(att_gateE if t < 2 else ()))

        def ep_out(t):
            par = t % 2
            ls = ln_scr2[t % 2]
            lsb = bt(f"ln_scr2_{t % 2}")
            emit_ln(ls, lsb, ps_pj[par], bt(f"ps_pj{par}"))
            sp = t % 2
            pl.op("vector", lambda eng, ls=ls, sp=sp, par=par:
                  eng.tensor_scalar(
                      out=x2_stage[sp][:], in0=ps_pj[par][:],
                      scalar1=ls[:, 14:15], scalar2=ls[:, 15:16],
                      op0=ALU.mult, op1=ALU.add),
                  reads=[bt(f"ps_pj{par}"), lsb], writes=[bt(f"x2_stage{sp}")])
            pl.op("vector", lambda eng, par=par, sp=sp: eng.tensor_add(
                out=x2_stage[sp][:], in0=x2_stage[sp][:],
                in1=xr_tile[par][:]),
                reads=[bt(f"x2_stage{sp}"), bt(f"xr_tile{par}")],
                writes=[bt(f"x2_stage{sp}")])
            if t + 2 < NT_OWN:
                xr_load(t + 2)
            pl.op("gpsimd", lambda eng, t=t, sp=sp: eng.dma_start(
                out=x2_scr[t], in_=x2_stage[sp][:]),
                reads=[bt(f"x2_stage{sp}")], writes=[bt(f"x2_scr_t{t}")],
                dma="LD_G")

        ep_zn(0)
        ep_proj(0)
        gate_pj = [None]
        for t in range(NT_OWN):
            if t + 1 < NT_OWN:
                ep_zn(t + 1)
                ep_proj(t + 1)
                if t + 1 == NT_OWN - 1:
                    gate_pj[0] = dict(pl.cnt)
            ep_out(t)
        ps2e.close()
        if _CFG["phases"] == 2:
            pl.barrier()
            pl.finish(nc, sems)
            return nc

        pre_wload = dict(pl.cnt)
        att_gate = list(att_end.items())
        pj_done = list(gate_pj[0].items())
        for i in range(HC):
            g2 = pj_done if 16 <= i < 24 else att_gate
            pl.op("sync", lambda eng, i=i: eng.dma_start(
                out=w2q_sb[:, i, :], in_=w2q_scr[i]),
                reads=[bt("w2q_scr")], writes=[bt("w2q_sb")], dma="LD_S",
                extra=g2)
        pl.op("gpsimd", lambda eng: eng.dma_start(out=b1_row[:], in_=b1_in[:]),
              writes=[bt("b1_row")], dma="LD_G", extra=att_gate)
        pl.op("gpsimd", lambda eng: eng.dma_start(out=b2_row[:], in_=b2_in[:]),
              writes=[bt("b2_row")], dma="LD_G", extra=att_gate)
        w1_gate = att_gate + [zn_gates[8]]
        for i in range(EC):
            pl.op("sync", lambda eng, i=i: eng.dma_start(
                out=w1q_sb[:, i, :], in_=w1q_scr[i]),
                reads=[bt("w1q_scr")], writes=[bt("w1q_sb")], dma="LD_S",
                extra=w1_gate)
        for e in ("tensor", "vector", "scalar", "gpsimd"):
            pl._wait(e, pre_wload)

        ps3 = ExitStack()
        ps_h = [ps3.enter_context(nc.psum_tensor(f"ps_h{i}", [P, 512], F32))
                for i in range(2)]
        ps_t3 = [ps3.enter_context(nc.psum_tensor(f"ps_t3{i}", [P, E], BF16))
                 for i in range(2)]
        ps_o2 = [ps3.enter_context(nc.psum_tensor(f"ps_o2{i}", [P, E], F32))
                 for i in range(2)]

        ep_gate = list(pre_wload.items())

        def x2_load(t):
            par = t % 2
            pl.op("sync", lambda eng, t=t, par=par: eng.dma_start(
                out=x2_tile[par][:], in_=x2_scr[t]),
                reads=[bt(f"x2_scr_t{t}")], writes=[bt(f"x2_tile{par}")],
                dma="LD_S", extra=[zn_gates[8 + par]])

        pl.op("gpsimd", lambda eng: eng.memset(thr_col[:, 0:1], -2.25),
              writes=[bt("thr_col")])
        pl.op("gpsimd", lambda eng: eng.memset(thr_col[:, 1:2], -6.25),
              writes=[bt("thr_col")])

        x2_load(0)
        x2_load(1)

        def emit_round(src_ap, sb, int_ap, ib, d_ap, db, sq_ap, qb,
                       r_ap, rb):
            pl.op("vector", lambda eng: eng.tensor_copy(out=int_ap, in_=src_ap),
                  reads=[sb], writes=[ib])
            pl.op("vector", lambda eng: eng.scalar_tensor_tensor(
                out=d_ap, in0=int_ap, scalar=-1.0, in1=src_ap,
                op0=ALU.mult, op1=ALU.add),
                reads=[ib, sb], writes=[db])
            pl.op("scalar", lambda eng: eng.activation(
                out=sq_ap, in_=d_ap, func=AF.Square),
                reads=[db], writes=[qb])
            pl.op("scalar", lambda eng: eng.activation(
                out=d_ap, in_=d_ap, func=AF.Sign),
                reads=[db], writes=[db])
            pl.op("vector", lambda eng: eng.tensor_single_scalar(
                out=src_ap, in_=sq_ap, scalar=0.25, op=ALU.is_ge),
                reads=[qb], writes=[sb])
            pl.op("vector", lambda eng: eng.scalar_tensor_tensor(
                out=src_ap, in0=sq_ap, scalar=2.25, in1=src_ap,
                op0=ALU.is_ge, op1=ALU.add),
                reads=[qb, sb], writes=[sb])
            pl.op("vector", lambda eng: eng.scalar_tensor_tensor(
                out=src_ap, in0=sq_ap, scalar=6.25, in1=src_ap,
                op0=ALU.is_ge, op1=ALU.add),
                reads=[qb, sb], writes=[sb])
            pl.op("vector", lambda eng: eng.tensor_mul(
                out=sq_ap, in0=src_ap, in1=d_ap),
                reads=[sb, db], writes=[qb])
            pl.op("vector", lambda eng: eng.tensor_add(
                out=r_ap, in0=int_ap, in1=sq_ap),
                reads=[ib, qb], writes=[rb])

        def stage_x(t):
            par = t % 2
            xb = bt(f"x2_tile{par}")
            ls = ln_scr3[t % 2]
            lsb = bt(f"ln_scr3_{t % 2}")
            emit_ln(ls, lsb, x2_tile[par], xb)
            pl.op("vector", lambda eng, par=par, ls=ls: eng.tensor_scalar(
                out=f1k[0][:], in0=x2_tile[par][:],
                scalar1=ls[:, 14:15], scalar2=ls[:, 15:16],
                op0=ALU.mult, op1=ALU.add),
                reads=[xb, lsb], writes=[bt("f1k0")])
            pl.op("vector", lambda eng: eng.reduce_max(
                out=hs_col[:, 0:1], in_=f1k[0][:], axis=AX,
                apply_absolute_value=True),
                reads=[bt("f1k0")], writes=[bt("hs_col")])
            pl.op("vector", lambda eng: eng.tensor_scalar_max(
                out=hs_col[:, 0:1], in0=hs_col[:, 0:1], scalar1=EPS),
                reads=[bt("hs_col")], writes=[bt("hs_col")])
            pl.op("vector", lambda eng: eng.reciprocal(
                out=hs_col[:, 1:2], in_=hs_col[:, 0:1]),
                reads=[bt("hs_col")], writes=[bt("hs_col")])
            pl.op("vector", lambda eng: eng.tensor_scalar_mul(
                out=hs_col[:, 1:2], in0=hs_col[:, 1:2], scalar1=QB),
                reads=[bt("hs_col")], writes=[bt("hs_col")])
            pl.op("vector", lambda eng: eng.tensor_scalar_mul(
                out=hs_col[:, 2:3], in0=hs_col[:, 0:1], scalar1=1.0 / QB),
                reads=[bt("hs_col")], writes=[bt("hs_col")])
            pl.op("vector", lambda eng: eng.tensor_scalar(
                out=f1k[1][:], in0=f1k[0][:], scalar1=hs_col[:, 1:2],
                scalar2=None, op0=ALU.mult),
                reads=[bt("f1k0"), bt("hs_col")], writes=[bt("f1k1")])
            emit_round(f1k[1][:], bt("f1k1"), xqi_t[:], bt("xqi"),
                       f1k[2][:], bt("f1k2"), f1k[0][:], bt("f1k0"),
                       f1k[1][:], bt("f1k1"))
            pl.op("vector", lambda eng: eng.tensor_scalar(
                out=xq_bf[:], in0=f1k[1][:], scalar1=hs_col[:, 2:3],
                scalar2=None, op0=ALU.mult),
                reads=[bt("f1k1"), bt("hs_col")], writes=[bt("xq_bf")])
            if t == 0 and _CFG["debug_scratch"]:
                pl.op("gpsimd", lambda eng: eng.dma_start(
                    out=dbg_xq[:], in_=xq_bf[:]),
                    reads=[bt("xq_bf")], writes=[bt("dbg")], dma="LD_G")
            tp = t % 2
            pst = bt(f"ps_t3{tp}")
            pl.group("tensor", [
                (lambda eng, e_=e_, tp=tp: eng.transpose(
                    ps_t3[tp][:, e_ * P:(e_ + 1) * P],
                    xq_bf[:, e_ * P:(e_ + 1) * P], ident[:]))
                for e_ in range(EC)],
                reads=[bt("xq_bf"), bt("ident")], writes=[pst])
            pl.op("scalar", lambda eng, tp=tp: eng.copy(
                out=xq_fm[:],
                in_=ps_t3[tp].rearrange("p (c q) -> p c q", c=EC)),
                reads=[pst], writes=[bt("xq_fm")])
            if t == 0 and _CFG["debug_scratch"]:
                pl.op("gpsimd", lambda eng: eng.dma_start(
                    out=dbg_xqfm[:], in_=xq_fm[:]),
                    reads=[bt("xq_fm")], writes=[bt("dbg")], dma="LD_G")

        def stage_hmm(t):
            for hg in range(8):
                hp = hg % 2
                psh = bt(f"ps_h{hp}")
                builds = [
                    (lambda eng, e_=e_, hg=hg, hp=hp: eng.matmul(
                        ps_h[hp][:], xq_fm[:, e_, :],
                        w1q_sb[:, e_, hg * 512:(hg + 1) * 512],
                        start=(e_ == 0), stop=False))
                    for e_ in range(EC)]
                builds.append(lambda eng, hg=hg, hp=hp: eng.matmul(
                    ps_h[hp][:], ones_row_bf[:],
                    b1_row[0:1, hg * 512:(hg + 1) * 512],
                    start=False, stop=True))
                pl.group("tensor", builds,
                         reads=[bt("xq_fm"), bt("w1q_sb"), bt("b1_row"),
                                bt("ones_row_bf")],
                         writes=[psh])
                pl.op("scalar", lambda eng, hg=hg, hp=hp: eng.activation(
                    out=h_sb[:, hg * 512:(hg + 1) * 512], in_=ps_h[hp][:],
                    func=AF.Gelu), reads=[psh], writes=[bt("h_sb")])

        def stage_rest(t):
            par = t % 2
            xb = bt(f"x2_tile{par}")
            pl.op("vector", lambda eng: eng.reduce_max(
                out=hs_col[:, 3:4], in_=h_sb[:], axis=AX,
                apply_absolute_value=True),
                reads=[bt("h_sb")], writes=[bt("hs_col")])
            pl.op("vector", lambda eng: eng.tensor_scalar_max(
                out=hs_col[:, 3:4], in0=hs_col[:, 3:4], scalar1=EPS),
                reads=[bt("hs_col")], writes=[bt("hs_col")])
            pl.op("vector", lambda eng: eng.reciprocal(
                out=hs_col[:, 4:5], in_=hs_col[:, 3:4]),
                reads=[bt("hs_col")], writes=[bt("hs_col")])
            pl.op("vector", lambda eng: eng.tensor_scalar_mul(
                out=hs_col[:, 4:5], in0=hs_col[:, 4:5], scalar1=QB),
                reads=[bt("hs_col")], writes=[bt("hs_col")])
            pl.op("vector", lambda eng: eng.tensor_scalar_mul(
                out=hs_col[:, 5:6], in0=hs_col[:, 3:4], scalar1=1.0 / QB),
                reads=[bt("hs_col")], writes=[bt("hs_col")])
            op_ = t % 2
            pso = bt(f"ps_o2{op_}")
            for qtr in range(4):
                lo = qtr * E
                pl.op("vector", lambda eng, lo=lo: eng.tensor_scalar(
                    out=f1k[1][:], in0=h_sb[:, lo:lo + E],
                    scalar1=hs_col[:, 4:5], scalar2=None, op0=ALU.mult),
                    reads=[bt("h_sb"), bt("hs_col")], writes=[bt("f1k1")])
                emit_round(f1k[1][:], bt("f1k1"), xqi_t[:], bt("xqi"),
                           f1k[2][:], bt("f1k2"), f1k[0][:], bt("f1k0"),
                           f1k[1][:], bt("f1k1"))
                pl.op("vector", lambda eng: eng.tensor_scalar(
                    out=hq_bf[:], in0=f1k[1][:], scalar1=hs_col[:, 5:6],
                    scalar2=None, op0=ALU.mult),
                    reads=[bt("f1k1"), bt("hs_col")], writes=[bt("hq_bf")])
                tp2 = qtr % 2
                pst2 = bt(f"ps_t3{tp2}")
                pl.group("tensor", [
                    (lambda eng, e_=e_, tp2=tp2: eng.transpose(
                        ps_t3[tp2][:, e_ * P:(e_ + 1) * P],
                        hq_bf[:, e_ * P:(e_ + 1) * P], ident[:]))
                    for e_ in range(EC)],
                    reads=[bt("hq_bf"), bt("ident")], writes=[pst2])
                pl.op("scalar", lambda eng, qtr=qtr, tp2=tp2: eng.copy(
                    out=hq_fm[:, qtr * 8:(qtr + 1) * 8, :],
                    in_=ps_t3[tp2].rearrange("p (c q) -> p c q", c=EC)),
                    reads=[pst2], writes=[bt("hq_fm")])
                builds = []
                for e_ in range(8):
                    hc = qtr * 8 + e_
                    for half in range(2):
                        builds.append(
                            lambda eng, hc=hc, half=half, op_=op_:
                            eng.matmul(
                                ps_o2[op_][:, half * 512:(half + 1) * 512],
                                hq_fm[:, hc, :],
                                w2q_sb[:, hc, half * 512:(half + 1) * 512],
                                start=(hc == 0), stop=False))
                pl.group("tensor", builds,
                         reads=[bt("hq_fm"), bt("w2q_sb")], writes=[pso])
            if t == 0 and _CFG["debug_scratch"]:
                pl.op("gpsimd", lambda eng: eng.dma_start(
                    out=dbg_h[:], in_=h_sb[:]),
                    reads=[bt("h_sb")], writes=[bt("dbg")], dma="LD_G")
                pl.op("gpsimd", lambda eng: eng.dma_start(
                    out=dbg_hqfm[:], in_=hq_fm[:]),
                    reads=[bt("hq_fm")], writes=[bt("dbg")], dma="LD_G")
            pl.group("tensor", [
                (lambda eng, half=half, op_=op_: eng.matmul(
                    ps_o2[op_][:, half * 512:(half + 1) * 512],
                    ones_row_bf[:], b2_row[0:1, half * 512:(half + 1) * 512],
                    start=False, stop=True))
                for half in range(2)],
                reads=[bt("b2_row"), bt("ones_row_bf")], writes=[pso])
            ls4 = ln_scr3[t % 2]
            lsb4 = bt(f"ln_scr3_{t % 2}")
            pl.op("vector", lambda eng, op_=op_, ls=ls4: eng.bn_stats(
                out=ls[:, 0:6], in_=ps_o2[op_][:, 0:512]),
                reads=[pso], writes=[lsb4])
            pl.op("vector", lambda eng, op_=op_, ls=ls4: eng.bn_stats(
                out=ls[:, 6:12], in_=ps_o2[op_][:, 512:1024]),
                reads=[pso], writes=[lsb4])
            pl.op("vector", lambda eng, ls=ls4: eng.bn_aggr(
                out=ls[:, 12:14], in_=ls[:, 0:12]),
                reads=[lsb4], writes=[lsb4])
            pl.op("vector", lambda eng, ls=ls4: eng.tensor_scalar_add(
                out=ls[:, 14:15], in0=ls[:, 13:14], scalar1=EPS),
                reads=[lsb4], writes=[lsb4])
            pl.op("scalar", lambda eng, ls=ls4: eng.activation(
                out=ls[:, 14:15], in_=ls[:, 14:15], func=AF.Sqrt),
                reads=[lsb4], writes=[lsb4])
            pl.op("vector", lambda eng, ls=ls4: eng.reciprocal(
                out=ls[:, 14:15], in_=ls[:, 14:15]),
                reads=[lsb4], writes=[lsb4])
            pl.op("vector", lambda eng, ls=ls4: eng.scalar_tensor_tensor(
                out=ls[:, 15:16], in0=ls[:, 12:13], scalar=-1.0,
                in1=ls[:, 14:15], op0=ALU.mult, op1=ALU.mult),
                reads=[lsb4], writes=[lsb4])
            if t == 0 and _CFG["debug_scratch"]:
                pl.op("gpsimd", lambda eng: eng.dma_start(
                    out=dbg_ls[:], in_=ln_scr3[0][:]),
                    reads=[bt("ln_scr3_0")], writes=[bt("dbg")], dma="LD_G")
                pl.op("vector", lambda eng, op_=op_: eng.tensor_copy(
                    out=f1k[0][:], in_=ps_o2[op_][:]),
                    reads=[pso], writes=[bt("f1k0")])
                pl.op("gpsimd", lambda eng: eng.dma_start(
                    out=dbg_o2[:], in_=f1k[0][:]),
                    reads=[bt("f1k0")], writes=[bt("dbg")], dma="LD_G")
            sp = 0
            ysb = bt("y_stage0")
            pl.op("vector", lambda eng, op_=op_, ls=ls4, sp=sp:
                  eng.tensor_scalar(
                      out=y_stage[sp][:], in0=ps_o2[op_][:],
                      scalar1=ls[:, 14:15], scalar2=ls[:, 15:16],
                      op0=ALU.mult, op1=ALU.add),
                  reads=[pso, lsb4], writes=[ysb])
            pl.op("vector", lambda eng, par=par, sp=sp: eng.tensor_add(
                out=y_stage[sp][:], in0=y_stage[sp][:], in1=x2_tile[par][:]),
                reads=[ysb, xb], writes=[ysb])
            if t + 2 < NT_OWN:
                x2_load(t + 2)
            pl.op("scalar", lambda eng, t=t, sp=sp: eng.dma_start(
                out=y_out[t * P:(t + 1) * P, :], in_=y_stage[sp][:]),
                reads=[ysb], writes=[bt("y_out")], dma="LD_A")

        stage_x(0)
        for t in range(NT_OWN):
            stage_hmm(t)
            if t + 1 < NT_OWN:
                stage_x(t + 1)
            stage_rest(t)

        ps3.close()
        pl.barrier()
        pl.finish(nc, sems)
    return nc


_NC_CACHE = {}


def kernel(L0_attn_x, L0_attn_Wq, L0_attn_bq, L0_attn_Wk, L0_attn_bk,
           L0_attn_Wv, L0_attn_bv, L0_ffn_W1, L0_ffn_b1, L0_ffn_W2, L0_ffn_b2):
    x = np.ascontiguousarray(
        np.asarray(L0_attn_x, dtype=np.float32).reshape(S_TOT, E))
    ins = {
        "wqt": np.ascontiguousarray(np.asarray(L0_attn_Wq, np.float32)),
        "wkt": np.ascontiguousarray(np.asarray(L0_attn_Wk, np.float32)),
        "wvt": np.ascontiguousarray(np.asarray(L0_attn_Wv, np.float32).T),
        "w1t": np.ascontiguousarray(np.asarray(L0_ffn_W1, np.float32).T),
        "w2t": np.ascontiguousarray(np.asarray(L0_ffn_W2, np.float32).T),
        "bq_in": np.ascontiguousarray(np.asarray(L0_attn_bq, np.float32)),
        "bk_in": np.ascontiguousarray(np.asarray(L0_attn_bk, np.float32)),
        "bv_in": np.ascontiguousarray(np.asarray(L0_attn_bv, np.float32)),
        "b1_in": np.ascontiguousarray(np.asarray(L0_ffn_b1, np.float32)),
        "b2_in": np.ascontiguousarray(np.asarray(L0_ffn_b2, np.float32)),
    }
    if "prog" not in _NC_CACHE:
        _NC_CACHE["prog"] = build_program()
    nc = _NC_CACHE["prog"]

    in_maps = []
    for c in range(NCORES):
        m = dict(ins)
        m["x_in"] = np.ascontiguousarray(np.roll(x, -c * QOWN, axis=0))
        in_maps.append(m)
    res = run_bass_kernel_spmd(nc, in_maps, core_ids=list(range(NCORES)))
    out = np.concatenate([res.results[c]["y_out"] for c in range(NCORES)],
                         axis=0)
    return out.reshape(1, S_TOT, E).astype(np.float32)
